# version 45
# baseline (speedup 1.0000x reference)
import contextlib
import os
import numpy as np
import concourse.bass as bass
import concourse.mybir as mybir
from concourse.bass_utils import run_bass_kernel_spmd

F32 = mybir.dt.float32
F32R = mybir.dt.float32r
BF16 = mybir.dt.bfloat16
I32 = mybir.dt.int32
AF = mybir.ActivationFunctionType
ALU = mybir.AluOpType

C = 1024
G = 8
GH = 32
T = 8192
TO = 4096
B = 4
EPS = 1e-5

TT1 = 256
NT1 = T // TT1
TT2 = 256
NT2 = TO // TT2
TT3 = 256
NT3 = TO // TT3

IV_LN0W, IV_LN0B, IV_LN1W, IV_LN1B, IV_LN2W, IV_LN2B = 0, 1, 2, 3, 4, 5
IV_TD, IV_TF, IV_TMK, IV_TMV, IV_TMR, IV_CMK, IV_CMR, IV_B2 = 6, 7, 8, 9, 10, 11, 12, 13
NV = 14
DEBUG_TAPS = False


class Ctx:
    ENGS = ("pe", "v", "s", "pool", "sync")

    def __init__(self, nc):
        self.nc = nc
        self.es = contextlib.ExitStack()
        self.sem = {e: self.es.enter_context(nc.semaphore(f"sem_{e}"))
                    for e in self.ENGS}
        self.cnt = {e: 0 for e in self.ENGS}
        self.ev = {}
        self.wm = {}

    def mark(self, eng, instr, label=None, dma=False):
        assert not dma
        self.cnt[eng] += 1
        instr.then_inc(self.sem[eng], 1)
        if label is not None:
            self.ev[label] = (eng, self.cnt[eng])
        return self.cnt[eng]

    def mark_dma(self, stream, instr, label=None):
        key = "dma:" + stream
        if key not in self.sem:
            self.sem[key] = self.es.enter_context(
                self.nc.semaphore(f"sem_{stream}"))
            self.cnt[key] = 0
        self.cnt[key] += 16
        instr.then_inc(self.sem[key], 16)
        if label is not None:
            self.ev[label] = (key, self.cnt[key])
        return self.cnt[key]

    def need(self, consumer, engine_obj, label):
        prod, val = self.ev[label]
        if prod == consumer:
            return
        key = (consumer, prod)
        if self.wm.get(key, -1) >= val:
            return
        engine_obj.wait_ge(self.sem[prod], val)
        self.wm[key] = val

    def need_if(self, consumer, engine_obj, label):
        if label in self.ev:
            self.need(consumer, engine_obj, label)

    def dma_war(self, consumer, engine_obj, stream):
        key = "dma:" + stream
        if key not in self.cnt or self.cnt[key] == 0:
            return
        val = self.cnt[key]
        wkey = (consumer, key)
        if self.wm.get(wkey, -1) >= val:
            return
        engine_obj.wait_ge(self.sem[key], val)
        self.wm[wkey] = val


def build_program():
    nc = bass.Bass("TRN2", target_bir_lowering=False)

    xT = nc.dram_tensor("xT", [C, T], F32, kind="ExternalInput").ap()
    xTo = nc.dram_tensor("xTo", [C, TO], F32, kind="ExternalInput").ap()
    wkT = nc.dram_tensor("wkT", [C, C], F32, kind="ExternalInput").ap()
    wvT = nc.dram_tensor("wvT", [C, C], F32, kind="ExternalInput").ap()
    wrT = nc.dram_tensor("wrT", [C, C], F32, kind="ExternalInput").ap()
    woT = nc.dram_tensor("woT", [C, C], F32, kind="ExternalInput").ap()
    wfkT = nc.dram_tensor("wfkT", [C, 4 * C], F32, kind="ExternalInput").ap()
    wfvT = nc.dram_tensor("wfvT", [4 * C, C], F32, kind="ExternalInput").ap()
    wfrT = nc.dram_tensor("wfrT", [C, C], F32, kind="ExternalInput").ap()
    w1 = nc.dram_tensor("w1", [C, 4 * C], F32, kind="ExternalInput").ap()
    w2 = nc.dram_tensor("w2", [4 * C, C], F32, kind="ExternalInput").ap()
    pvec = nc.dram_tensor("pvec", [128, G, NV], F32, kind="ExternalInput").ap()
    pvecH = nc.dram_tensor("pvecH", [128, GH], F32, kind="ExternalInput").ap()
    hmask = nc.dram_tensor("hmask", [128, 1], F32, kind="ExternalInput").ap()
    outT = nc.dram_tensor("outT", [C, TO], F32, kind="ExternalOutput").ap()

    rwkvs = nc.dram_tensor("rwkvs", [C, T + 1], F32).ap()
    x1s = nc.dram_tensor("x1s", [C, T + 1], F32).ap()
    ydbg = nc.dram_tensor("ydbg", [C, T], BF16).ap() if DEBUG_TAPS else None
    ydbg_v = ydbg.rearrange("(g p) t -> p g t", p=128) if DEBUG_TAPS else None
    kdbg = nc.dram_tensor("kdbg", [C, T], BF16).ap() if DEBUG_TAPS else None
    kdbg_v = kdbg.rearrange("(g p) t -> p g t", p=128) if DEBUG_TAPS else None
    vdbg = nc.dram_tensor("vdbg", [C, T], BF16).ap() if DEBUG_TAPS else None
    vdbg_v = vdbg.rearrange("(g p) t -> p g t", p=128) if DEBUG_TAPS else None
    x1dbg = nc.dram_tensor("x1dbg", [C, T], F32R).ap() if DEBUG_TAPS else None
    x1dbg_v = x1dbg.rearrange("(g p) t -> p g t", p=128) if DEBUG_TAPS else None
    k2dbg = nc.dram_tensor("k2dbg", [C, TO], BF16).ap() if DEBUG_TAPS else None
    k2dbg_v = k2dbg.rearrange("(g p) t -> p g t", p=128) if DEBUG_TAPS else None
    kfdbg = nc.dram_tensor("kfdbg", [4 * C, TO], BF16).ap() if DEBUG_TAPS else None
    kfdbg_v = kfdbg.rearrange("(g p) t -> p g t", p=128) if DEBUG_TAPS else None
    rs = nc.dram_tensor("rs", [C, TO], F32).ap()
    wfv_b = nc.dram_tensor("wfv_b", [4 * C, C], BF16).ap()
    w2_b = nc.dram_tensor("w2_b", [4 * C, C], BF16).ap()

    rwkvs_v = rwkvs.rearrange("(g p) t -> p g t", p=128)
    x1s_v = x1s.rearrange("(g p) t -> p g t", p=128)
    rs_v = rs.rearrange("(g p) t -> p g t", p=128)
    outT_v = outT.rearrange("(g p) t -> p g t", p=128)
    xT_v = xT.rearrange("(g p) t -> p g t", p=128)
    xTo_v = xTo.rearrange("(g p) t -> p g t", p=128)

    cx = Ctx(nc)
    es = cx.es

    pv = es.enter_context(nc.sbuf_tensor("pv", [128, G, NV], F32))
    pvH = es.enter_context(nc.sbuf_tensor("pvH", [128, GH], F32))
    hm = es.enter_context(nc.sbuf_tensor("hm", [128, 1], F32))
    eu8 = es.enter_context(nc.sbuf_tensor("eu8", [128, G], F32))
    dec8 = es.enter_context(nc.sbuf_tensor("dec8", [128, G], F32))
    onesr = es.enter_context(nc.sbuf_tensor("onesr", [128, 1], F32R))
    onesf = es.enter_context(nc.sbuf_tensor("onesf", [128, 1], F32))
    onesrow = es.enter_context(nc.sbuf_tensor("onesrow", [1, 128], F32R))
    onesrowf = es.enter_context(nc.sbuf_tensor("onesrowf", [1, 128], F32))
    epsb = es.enter_context(nc.sbuf_tensor("epsb", [128, 1], F32))
    ROWN = 260
    rows = es.enter_context(nc.sbuf_tensor("rows", [1, 8 * ROWN], F32R))

    def row(idx):
        return rows[0:1, idx * ROWN:idx * ROWN + ROWN]

    pb = [es.enter_context(nc.psum_tensor(f"pb{i}", [128, 512], F32))
          for i in range(8)]

    hoff_reg = {}

    def pe_stats(t_eng, tag, rnd_ap, sq_ap, bank_x, bank_q, n, deps,
                 xoff=0, qoff=0):
        cx.need_if("pe", t_eng, "consts")
        for lbl in deps:
            cx.need_if("pe", t_eng, lbl)
        for kb in range(G):
            mm = t_eng.matmul(bank_x[0:1, xoff:xoff + n], onesr[:, 0:1],
                              rnd_ap[:, kb, :],
                              start=(kb == 0), stop=(kb == G - 1))
        cx.mark("pe", mm, f"stx_{tag}")
        for kb in range(G):
            mm = t_eng.matmul(bank_q[0:1, qoff:qoff + n], onesr[:, 0:1],
                              sq_ap[:, kb, :],
                              start=(kb == 0), stop=(kb == G - 1))
        cx.mark("pe", mm, f"st_{tag}")

    def v_rows(v_eng, tag, bank_x, bank_q, n, rbase, xoff=0, qoff=0,
               war_deps=()):
        cx.need("v", v_eng, f"st_{tag}")
        for lbl in war_deps:
            cx.need_if("v", v_eng, lbl)
        m_r, m2_r, ve_r = row(rbase), row(rbase + 1), row(rbase + 2)
        v_eng.tensor_scalar(m_r[0:1, 0:n], bank_x[0:1, xoff:xoff + n],
                            1.0 / C, None, ALU.mult)
        v_eng.tensor_tensor(m2_r[0:1, 0:n], m_r[0:1, 0:n], m_r[0:1, 0:n],
                            ALU.mult)
        ii = v_eng.scalar_tensor_tensor(ve_r[0:1, 0:n],
                                        bank_q[0:1, qoff:qoff + n],
                                        1.0 / C, m2_r[0:1, 0:n],
                                        ALU.mult, ALU.subtract)
        cx.mark("v", ii, f"vr_{tag}")

    def s_irow(s_eng, tag, n, rbase, war_deps=()):
        cx.need("s", s_eng, f"vr_{tag}")
        for lbl in war_deps:
            cx.need_if("s", s_eng, lbl)
        ve_r, i_r = row(rbase + 2), row(rbase + 3)
        s_eng.activation(i_r[0:1, 0:n], ve_r[0:1, 0:n], AF.Ln,
                         bias=epsb[0:1, 0:1])
        ii = s_eng.activation(i_r[0:1, 0:n], i_r[0:1, 0:n], AF.Exp, scale=-0.5)
        cx.mark("s", ii, f"ir_{tag}")

    def pe_bcast(t_eng, tag, bank_m, bank_i, n, rbase, war_deps=(),
                 moff=0, ioff=0):
        cx.need("pe", t_eng, f"vr_{tag}")
        cx.need("pe", t_eng, f"ir_{tag}")
        cx.need_if("pe", t_eng, "rows0")
        for lbl in war_deps:
            cx.need_if("pe", t_eng, lbl)
        t_eng.matmul(bank_m[:, moff:moff + n], onesrow[0:1, 0:128],
                     row(rbase)[0:1, 0:n], start=True, stop=True)
        ii = t_eng.matmul(bank_i[:, ioff:ioff + n], onesrow[0:1, 0:128],
                          row(rbase + 3)[0:1, 0:n], start=True, stop=True)
        cx.mark("pe", ii, f"bc_{tag}")

    def v_apply(v_eng, tag, bank_m, bank_i, x_ap, out_ap, n, war_deps=(),
                moff=0, ioff=0):
        cx.need("v", v_eng, f"bc_{tag}")
        for lbl in war_deps:
            cx.need_if("v", v_eng, lbl)
        mb = bank_m[:, moff:moff + n].unsqueeze(1).broadcast_to([128, G, n])
        ib = bank_i[:, ioff:ioff + n].unsqueeze(1).broadcast_to([128, G, n])
        v_eng.scalar_tensor_tensor(out_ap, mb, -1.0, x_ap, ALU.mult, ALU.add)
        ii = v_eng.tensor_tensor(out_ap, out_ap, ib, ALU.mult)
        cx.mark("v", ii, f"ap_{tag}")

    def v_apply_affine(v_eng, tag, bank_m, bank_i, x_ap, out_ap, n, iv_w,
                       iv_b, war_deps=(), moff=0, ioff=0):
        cx.need("v", v_eng, f"bc_{tag}")
        for lbl in war_deps:
            cx.need_if("v", v_eng, lbl)
        mb = bank_m[:, moff:moff + n].unsqueeze(1).broadcast_to([128, G, n])
        ib = bank_i[:, ioff:ioff + n].unsqueeze(1).broadcast_to([128, G, n])
        v_eng.scalar_tensor_tensor(out_ap, mb, -1.0, x_ap, ALU.mult, ALU.add)
        ii = v_eng.tensor_tensor(out_ap, out_ap, ib, ALU.mult)
        cx.mark("v", ii, f"ap_{tag}")
        for g in range(G):
            ii = v_eng.tensor_scalar(out_ap[:, g, :], out_ap[:, g, :],
                                     pv[:, g, iv_w:iv_w + 1],
                                     pv[:, g, iv_b:iv_b + 1], ALU.mult, ALU.add)
        cx.mark("v", ii, f"af_{tag}")

    def s_affine(s_eng, tag, z_ap, out_ap, iv_w, iv_b, war_deps=()):
        cx.need("s", s_eng, f"ap_{tag}")
        for lbl in war_deps:
            cx.need_if("s", s_eng, lbl)
        for g in range(G):
            ii = s_eng.activation(out_ap[:, g, :], z_ap[:, g, :], AF.Identity,
                                  bias=pv[:, g, iv_b:iv_b + 1],
                                  scale=pv[:, g, iv_w:iv_w + 1])
        cx.mark("s", ii, f"af_{tag}")

    with contextlib.ExitStack() as p1:
        wk_s = p1.enter_context(nc.sbuf_tensor("wk_s", [128, G, C], BF16))
        wv_s = p1.enter_context(nc.sbuf_tensor("wv_s", [128, G, C], BF16))
        wr_s = p1.enter_context(nc.sbuf_tensor("wr_s", [128, G, C], BF16))
        wo_s = p1.enter_context(nc.sbuf_tensor("wo_s", [128, G, C], BF16))
        xin = [p1.enter_context(nc.sbuf_tensor(f"xin{j}", [128, G, TT1], F32R))
               for j in range(2)]
        sqb = p1.enter_context(nc.sbuf_tensor("sqb", [128, G, TT1], F32R))
        sqb2 = p1.enter_context(nc.sbuf_tensor("sqb2", [128, G, TT1], F32R))
        x1r = [p1.enter_context(nc.sbuf_tensor(f"x1r{j}", [128, G, TT1], F32R))
               for j in range(3)]
        h_ext = p1.enter_context(nc.sbuf_tensor("h_ext", [128, G, TT1 + 1], F32))
        xk_bf = p1.enter_context(nc.sbuf_tensor("xk_bf", [128, G, TT1], BF16))
        xv_bf = p1.enter_context(nc.sbuf_tensor("xv_bf", [128, G, TT1], BF16))
        xr_bf = p1.enter_context(nc.sbuf_tensor("xr_bf", [128, G, TT1], BF16))
        ekq = p1.enter_context(nc.sbuf_tensor("ekq", [128, G, 2, TT1 + 1], F32))
        AB = p1.enter_context(nc.sbuf_tensor("AB", [128, G, 2, TT1 + 1], F32))
        er2b = p1.enter_context(nc.sbuf_tensor("er2b", [128, 2, TT1], F32))
        nd = p1.enter_context(nc.sbuf_tensor("nd", [128, 3, 2, TT1], F32))
        den2_b = p1.enter_context(nc.sbuf_tensor("den2_b", [128, 2, TT1], F32))
        rden_b = p1.enter_context(nc.sbuf_tensor("rden_b", [128, 2, TT1], F32))
        dbuf = p1.enter_context(nc.sbuf_tensor("dbuf", [128, TT1], F32))
        y_bf0 = p1.enter_context(nc.sbuf_tensor("y_bf0", [128, G, TT1], BF16))
        x2e = p1.enter_context(nc.sbuf_tensor("x2e", [128, G, TT1], F32))

        with nc.Block() as blk:
            @blk.sync
            def _(sy):
                ii = sy.dma_start(out=pv[:, :, :], in_=pvec[:, :, :])
                cx.mark_dma("ldpv", ii)
                ii = sy.dma_start(out=pvH[:, :], in_=pvecH[:, :])
                cx.mark_dma("ldpv", ii)
                ii = sy.dma_start(out=hm[:, :], in_=hmask[:, :])
                cx.mark_dma("ldpv", ii, "ld_pv")


            @blk.gpsimd
            def _(po):
                ii = po.dma_start(out=wk_s[:, :, :],
                                  in_=wkT.rearrange("(kb p) o -> p kb o", p=128))
                cx.mark_dma("wp1", ii)
                ii = po.dma_start(out=wv_s[:, :, :],
                                  in_=wvT.rearrange("(kb p) o -> p kb o", p=128))
                cx.mark_dma("wp1", ii)
                ii = po.dma_start(out=wr_s[:, :, :],
                                  in_=wrT.rearrange("(kb p) o -> p kb o", p=128))
                cx.mark_dma("wp1", ii)
                ii = po.dma_start(out=wo_s[:, :, :],
                                  in_=woT.rearrange("(kb p) o -> p kb o", p=128))
                cx.mark_dma("wp1", ii, "ld_w1p")
                ii = po.dma_start(out=wfv_b[:, :], in_=wfvT[:, :])
                cx.mark_dma("wcv1", ii, "cv_wfv")
                ii = po.dma_start(out=w2_b[:, :], in_=w2[:, :])
                cx.mark_dma("wcv2", ii, "cv_w2")
                for j in range(2):
                    ii = po.dma_start(out=xin[j][:, :, :],
                                      in_=xT_v[:, :, j * TT1:(j + 1) * TT1])
                    cx.mark_dma(f"xin{j % 2}", ii, f"ldx_{j}")
                ii = po.memset(onesf[:, :], 1.0)
                n0 = cx.mark("pool", ii)
                po.wait_ge(cx.sem["pool"], n0)
                po.tensor_copy(onesr[:, :], onesf[:, :])
                po.memset(onesrowf[0:1, :], 1.0)
                po.tensor_copy(onesrow[0:1, :], onesrowf[0:1, :])
                ii = po.memset(epsb[:, :], EPS)
                cx.mark("pool", ii, "consts")
                cx.ev["rows0"] = ("pool", cx.cnt["pool"])
                po.memset(h_ext[:, :, 0:1], 0.0)
                ii = po.memset(AB[:, :, :, TT1:TT1 + 1], 0.0)
                cx.mark("pool", ii, "zeros")

            @blk.scalar
            def _(s):
                s.add_instruction(mybir.InstLoadActFuncSet(
                    name=nc.get_next_instruction_name(),
                    act_func_set_id=6, ins=[], outs=[]))
                cx.need("s", s, "ld_pv")
                s.activation(eu8[:, :], pv[:, :, IV_TF], AF.Exp)
                ii = s.activation(dec8[:, :], pv[:, :, IV_TD], AF.Exp)
                cx.mark("s", ii, "dec_a")

            @blk.vector
            def _(v):
                cx.need("v", v, "ld_pv")
                cx.need("v", v, "dec_a")
                cx.need("v", v, "consts")
                v.tensor_copy(
                    rows.rearrange("o (s w) -> o s w", w=ROWN)[0:1, :, 257:260],
                    onesrowf[0:1, 0:1].unsqueeze(1).broadcast_to([1, 8, 3]))
                ii = v.tensor_scalar(dec8[:, :], dec8[:, :], -1.0, None, ALU.mult)
                cx.mark("v", ii, "dec_b")

            @blk.scalar
            def _(s):
                cx.need("s", s, "dec_b")
                ii = s.activation(dec8[:, :], dec8[:, :], AF.Exp)
                cx.mark("s", ii, "dec_c")

            @blk.vector
            def _(v):
                cx.need("v", v, "dec_c")
                ii = v.engine_nop()
                cx.mark("v", ii, "dec_ext")

            @blk.scalar
            def _(s):
                cx.need("s", s, "zeros")
                ii = s.dma_start(out=rwkvs_v[:, :, 0:1], in_=h_ext[:, :, 0:1],
                                 allow_slow_non_contiguous=True)
                cx.mark_dma("zc", ii)
                ii = s.dma_start(out=x1s_v[:, :, 0:1], in_=h_ext[:, :, 0:1],
                                 allow_slow_non_contiguous=True)
                cx.mark_dma("zc", ii, "zc_cols")

            for i in range(-3, NT1 + 1):
                if 0 <= i - 1 < NT1:
                    for go in range(G):
                        @blk.tensor
                        def _(t, i=i, go=go):
                            bank = pb[2 + go % 5]
                            if go == 0:
                                cx.need("pe", t, f"y_{i - 1}")
                            if go < 2:
                                cx.need_if("pe", t, f"eka_{i - 1}_{6 + go}")
                            elif go < 4:
                                cx.need_if("pe", t, f"ekv_{i - 1}_{4 + go}")
                            elif go == 4:
                                cx.need_if("pe", t, f"erc_{i - 1}_{G - 1}")
                            else:
                                cx.need("pe", t, f"woev_{i - 1}_{go - 5}")
                            for kb in range(G):
                                mm = t.matmul(bank[:, 0:TT1],
                                              wo_s[:, kb, go * 128:(go + 1) * 128],
                                              y_bf0[:, kb, :],
                                              start=(kb == 0), stop=(kb == G - 1))
                            cx.mark("pe", mm, f"wo_{i - 1}_{go}")

                        @blk.scalar
                        def _(s, i=i, go=go):
                            cx.need("s", s, f"wo_{i - 1}_{go}")
                            if go == 0:
                                cx.dma_war("s", s, "x2st")
                            ii = s.activation(x2e[:, go, :],
                                              pb[2 + go % 5][:, 0:TT1],
                                              AF.Copy)
                            cx.mark("s", ii, f"woev_{i - 1}_{go}")

                    @blk.scalar
                    def _(s, i=i):
                        cx.need("s", s, f"woev_{i - 1}_{G - 1}")
                        ii = s.dma_start(
                            out=rwkvs_v[:, :, 1 + (i - 1) * TT1:1 + i * TT1],
                            in_=x2e[:, :, :])
                        cx.mark_dma("x2st", ii, f"x2st_{i - 1}")

                if 0 <= i + 1 < NT1:
                    j1 = i + 1
                    tag1 = f"l1_{j1}"

                    @blk.tensor
                    def _(t, j1=j1, tag1=tag1):
                        pe_bcast(t, tag1, pb[0], pb[0], TT1, 0,
                                 war_deps=[f"ap_l0_{j1}"], moff=0, ioff=256)

                    @blk.vector
                    def _(v, j1=j1, tag1=tag1):
                        war = [f"hc_{j1}"]
                        if j1 == 0:
                            war.append("zeros")
                        v_apply(v, tag1, pb[0], pb[0], x1r[j1 % 3][:, :, :],
                                h_ext[:, :, 1:TT1 + 1], TT1, war_deps=war,
                                moff=0, ioff=256)

                def emit_aff1(i=i):
                    if 0 <= i + 1 < NT1:
                        tag1 = f"l1_{i + 1}"

                        @blk.scalar
                        def _(s, tag1=tag1):
                            s_affine(s, tag1, h_ext[:, :, 1:TT1 + 1],
                                     h_ext[:, :, 1:TT1 + 1], IV_LN1W, IV_LN1B)

                if 0 <= i < NT1:
                    @blk.vector
                    def _(v, i=i):
                        if i == 0:
                            cx.need("v", v, "zeros")
                            cx.need("v", v, "dec_ext")
                        if DEBUG_TAPS:
                            cx.need_if("v", v, f"ydbg_{i - 1}")
                        ii = v.tensor_copy(ekq[:, :, :, 0:1],
                                           AB[:, :, :, TT1:TT1 + 1])
                        cx.mark("v", ii, f"carry_{i}")

                if 0 <= i + 3 < NT1:
                    j0e = i + 3

                    @blk.gpsimd
                    def _(po, j0e=j0e):
                        cx.need("pool", po, f"ldx_{j0e}")
                        cx.need_if("pool", po, f"st_l0_{j0e - 1}")
                        ii = po.tensor_tensor(sqb[:, :, :], xin[j0e % 2][:, :, :],
                                              xin[j0e % 2][:, :, :], ALU.mult)
                        cx.mark("pool", ii, f"sq0_{j0e}")

                def emit_b0a0(i=i):
                    if 0 <= i + 2 < NT1:
                        tag0b = f"l0_{i + 2}"

                        @blk.tensor
                        def _(t, i=i, tag0b=tag0b):
                            pe_bcast(t, tag0b, pb[0], pb[0], TT1, 4,
                                     war_deps=[f"ap_l1_{i + 1}"], moff=0,
                                     ioff=256)

                        @blk.vector
                        def _(v, i=i, tag0b=tag0b):
                            j0 = i + 2
                            v_apply(v, tag0b, pb[0], pb[0],
                                    xin[j0 % 2][:, :, :],
                                    xin[j0 % 2][:, :, :], TT1,
                                    war_deps=[f"sq0_{j0}", f"st_l0_{j0}"],
                                    moff=0, ioff=256)

                def emit_aff0(i=i):
                    if 0 <= i + 2 < NT1:
                        j0 = i + 2
                        tag0 = f"l0_{j0}"

                        @blk.scalar
                        def _(s, j0=j0, tag0=tag0):
                            war0 = [f"st_l1_{j0 - 3}"]
                            if DEBUG_TAPS:
                                war0.append(f"x1dbg_{j0 - 3}")
                            cx.dma_war("s", s, "x1st")
                            s_affine(s, tag0, xin[j0 % 2][:, :, :],
                                     x1r[j0 % 3][:, :, :], IV_LN0W, IV_LN0B,
                                     war_deps=war0)
                            cx.need_if("s", s, f"st_l1_{j0 - 1}")
                            ii = s.activation(sqb2[:, :, :],
                                              x1r[j0 % 3][:, :, :], AF.Square)
                            cx.mark("s", ii, f"sq1_{j0}")

                        @blk.gpsimd
                        def _(po, j0=j0, tag0=tag0):
                            cx.need("pool", po, f"af_{tag0}")
                            ii = po.dma_start(
                                out=x1s_v[:, :, 1 + j0 * TT1:1 + (j0 + 1) * TT1],
                                in_=x1r[j0 % 3][:, :, :])
                            cx.mark_dma("x1st", ii, f"x1st_{j0}")

                    @blk.gpsimd
                    def _(po, i=i):
                        if 2 <= i + 4 < NT1:
                            j = i + 4
                            cx.need_if("pool", po, f"af_l0_{j - 2}")
                            cx.dma_war("pool", po, f"xin{j % 2}")
                            ii = po.dma_start(
                                out=xin[j % 2][:, :, :],
                                in_=xT_v[:, :, j * TT1:(j + 1) * TT1])
                            cx.mark_dma(f"xin{j % 2}", ii, f"ldx_{j}")

                def emit_st1(i=i):
                    if 0 <= i + 2 < NT1:
                        j0 = i + 2
                        tag0 = f"l0_{j0}"
                        tag1b = f"l1_{j0}"

                        @blk.tensor
                        def _(t, j0=j0, tag0=tag0, tag1b=tag1b):
                            pe_stats(t, tag1b, x1r[j0 % 3][:, :, :],
                                     sqb2[:, :, :], pb[1], pb[1], TT1,
                                     deps=[f"af_{tag0}", f"sq1_{j0}",
                                           f"vr_l1_{j0 - 1}"], xoff=0,
                                     qoff=256)

                def emit_st0(i=i):
                    if 0 <= i + 3 < NT1:
                        j0 = i + 3
                        tag0c = f"l0_{j0}"

                        @blk.tensor
                        def _(t, j0=j0, tag0c=tag0c):
                            pe_stats(t, tag0c, xin[j0 % 2][:, :, :],
                                     sqb[:, :, :], pb[7], pb[7], TT1,
                                     deps=[f"ldx_{j0}", f"sq0_{j0}",
                                           f"vr_l0_{j0 - 1}"], xoff=0,
                                     qoff=256)

                if 0 <= i < NT1:
                    for g in range(G):
                        if g == 2:
                            emit_aff1(i)
                        if g == 4:
                            emit_b0a0(i)
                        if g == 5:
                            emit_aff0(i)

                        @blk.tensor
                        def _(t, i=i, g=g):
                            if i == 0 and g == 0:
                                cx.need("pe", t, "ld_w1p")
                            cx.need("pe", t, f"mx_{i}")
                            bk, bv, br = pb[2 + g % 2], pb[4 + g % 2], pb[6]
                            if g < 2:
                                cx.need_if("pe", t, f"woev_{i - 1}_{5 + g}")
                            cx.need_if("pe", t, f"eka_{i}_{g - 2}")
                            for kb in range(G):
                                mm = t.matmul(bk[:, 0:TT1],
                                              wk_s[:, kb, g * 128:(g + 1) * 128],
                                              xk_bf[:, kb, :], start=(kb == 0),
                                              stop=(kb == G - 1))
                            cx.mark("pe", mm, f"k_{i}_{g}")
                            if g == 0:
                                cx.need_if("pe", t, f"woev_{i - 1}_{7}")
                                cx.need_if("pe", t, f"woev_{i - 1}_{4}")
                            if g == 1:
                                cx.need_if("pe", t, f"woev_{i - 1}_{3}")
                            cx.need_if("pe", t, f"ekv_{i}_{g - 2}")
                            for kb in range(G):
                                mm = t.matmul(bv[:, 0:TT1],
                                              wv_s[:, kb, g * 128:(g + 1) * 128],
                                              xv_bf[:, kb, :], start=(kb == 0),
                                              stop=(kb == G - 1))
                            cx.mark("pe", mm, f"vv_{i}_{g}")
                            cx.need_if("pe", t, f"erc_{i}_{g - 1}")
                            for kb in range(G):
                                mm = t.matmul(br[:, 0:TT1],
                                              wr_s[:, kb, g * 128:(g + 1) * 128],
                                              xr_bf[:, kb, :], start=(kb == 0),
                                              stop=(kb == G - 1))
                            cx.mark("pe", mm, f"r_{i}_{g}")

                        @blk.scalar
                        def _(s, i=i, g=g):
                            cx.need("s", s, f"k_{i}_{g}")
                            if g == 0:
                                cx.need_if("s", s, f"y_{i - 1}")
                                cx.need_if("s", s, f"carry_{i}")
                            ii = s.activation(ekq[:, g, 1, 1:TT1 + 1],
                                              pb[2 + g % 2][:, 0:TT1], AF.Exp)
                            cx.mark("s", ii, f"eka_{i}_{g}")
                            cx.need("s", s, f"r_{i}_{g}")
                            cx.need_if("s", s, f"dn_{i}_{g - 2}")
                            ii = s.activation(er2b[:, g % 2, :], pb[6][:, 0:TT1],
                                              AF.Exp, scale=-1.0)
                            cx.mark("s", ii, f"erc_{i}_{g}")
                            if g >= 1:
                                gq = g - 1
                                cx.need("s", s, f"dn_{i}_{gq}")
                                cx.need_if("s", s, f"yg_{i}_{gq - 1}")
                                s.activation(den2_b[:, gq % 2, :],
                                             den2_b[:, gq % 2, :], AF.Ln)
                                ii = s.activation(rden_b[:, gq % 2, :],
                                                  den2_b[:, gq % 2, :],
                                                  AF.Exp, scale=-1.0)
                                cx.mark("s", ii, f"rdn_{i}_{gq}")

                        @blk.vector
                        def _(v, i=i, g=g):
                            cx.need("v", v, f"eka_{i}_{g}")
                            cx.need("v", v, f"vv_{i}_{g}")
                            ii = v.tensor_tensor(ekq[:, g, 0, 1:TT1 + 1],
                                                 ekq[:, g, 1, 1:TT1 + 1],
                                                 pb[4 + g % 2][:, 0:TT1], ALU.mult)
                            cx.mark("v", ii, f"ekv_{i}_{g}")
                            dec_b = dec8[:, g:g + 1].broadcast_to([128, TT1 + 1])
                            v.tensor_tensor_scan(AB[:, g, 0, :], dec_b,
                                                 ekq[:, g, 0, :], 0.0,
                                                 ALU.mult, ALU.add)
                            ii = v.tensor_tensor_scan(AB[:, g, 1, :], dec_b,
                                                      ekq[:, g, 1, :], 0.0,
                                                      ALU.mult, ALU.add)
                            cx.mark("v", ii, f"scn_{i}_{g}")
                            cx.need_if("v", v, f"yg_{i}_{g - 3}")
                            ii = v.scalar_tensor_tensor(nd[:, g % 3, :, :],
                                                        ekq[:, g, :, 1:TT1 + 1],
                                                        eu8[:, g:g + 1],
                                                        AB[:, g, :, 0:TT1],
                                                        ALU.mult, ALU.add)
                            cx.mark("v", ii, f"nd_{i}_{g}")
                            cx.need("v", v, f"erc_{i}_{g}")
                            cx.need_if("v", v, f"rdn_{i}_{g - 2}")
                            ii = v.scalar_tensor_tensor(den2_b[:, g % 2, :],
                                                        er2b[:, g % 2, :], 1.0,
                                                        nd[:, g % 3, 1, :],
                                                        ALU.add, ALU.mult)
                            cx.mark("v", ii, f"dn_{i}_{g}")

                        @blk.gpsimd
                        def _(po, i=i, g=g):
                            if g >= 1:
                                gq = g - 1
                                cx.need("pool", po, f"rdn_{i}_{gq}")
                                if gq == 0:
                                    cx.need_if("pool", po,
                                               f"wo_{i - 1}_{G - 1}")
                                ii = po.tensor_tensor(y_bf0[:, gq, :],
                                                      nd[:, gq % 3, 0, :],
                                                      rden_b[:, gq % 2, :],
                                                      ALU.mult)
                                cx.mark("pool", ii, f"yg_{i}_{gq}")

                    @blk.scalar
                    def _(s, i=i):
                        gq = G - 1
                        cx.need("s", s, f"dn_{i}_{gq}")
                        cx.need_if("s", s, f"yg_{i}_{gq - 1}")
                        s.activation(den2_b[:, gq % 2, :],
                                     den2_b[:, gq % 2, :], AF.Ln)
                        ii = s.activation(rden_b[:, gq % 2, :],
                                          den2_b[:, gq % 2, :],
                                          AF.Exp, scale=-1.0)
                        cx.mark("s", ii, f"rdn_{i}_{gq}")

                    @blk.gpsimd
                    def _(po, i=i):
                        gq = G - 1
                        cx.need("pool", po, f"rdn_{i}_{gq}")
                        ii = po.tensor_tensor(y_bf0[:, gq, :],
                                              nd[:, gq % 3, 0, :],
                                              rden_b[:, gq % 2, :],
                                              ALU.mult)
                        cx.mark("pool", ii, f"yg_{i}_{gq}")
                        cx.ev[f"y_{i}"] = ("pool", cx.cnt["pool"])

                if DEBUG_TAPS and 0 <= i < NT1:
                    @blk.scalar
                    def _(s, i=i):
                        cx.need("s", s, f"y_{i}")
                        ii = s.dma_start(out=ydbg_v[:, :, i * TT1:(i + 1) * TT1],
                                         in_=y_bf0[:, :, :])
                        cx.mark_dma("ydbg", ii, f"ydbg_{i}")
                        cx.need("s", s, f"mx_{i}")
                        ii = s.dma_start(out=kdbg_v[:, :, i * TT1:(i + 1) * TT1],
                                         in_=xk_bf[:, :, :])
                        cx.mark_dma("ydbg", ii)
                        ii = s.dma_start(out=vdbg_v[:, :, i * TT1:(i + 1) * TT1],
                                         in_=xv_bf[:, :, :])
                        cx.mark_dma("ydbg", ii, f"kvdbg_{i}")
                        cx.need("s", s, f"af_l0_{i}")
                        ii = s.dma_start(out=x1dbg_v[:, :, i * TT1:(i + 1) * TT1],
                                         in_=x1r[i % 3][:, :, :])
                        cx.mark_dma("ydbg", ii, f"x1dbg_{i}")

                if not (0 <= i < NT1):
                    emit_aff1(i)
                    emit_b0a0(i)
                    emit_aff0(i)

                emit_st1(i)
                emit_st0(i)

                if 0 <= i + 1 < NT1:
                    j1 = i + 1

                    @blk.vector
                    def _(v, i=i, j1=j1):
                        cx.need("v", v, f"af_l1_{j1}")
                        cx.need_if("v", v, f"r_{j1 - 1}_{G - 1}")
                        if DEBUG_TAPS:
                            cx.need_if("v", v, f"kvdbg_{j1 - 1}")
                        ii = None
                        for g in range(G):
                            sh_ap = h_ext[:, g, 0:TT1]
                            v.tensor_tensor(dbuf[:, :], h_ext[:, g, 1:TT1 + 1],
                                            sh_ap, ALU.subtract)
                            v.scalar_tensor_tensor(xk_bf[:, g, :], dbuf[:, :],
                                                   pv[:, g, IV_TMK:IV_TMK + 1],
                                                   sh_ap, ALU.mult, ALU.add)
                            v.scalar_tensor_tensor(xv_bf[:, g, :], dbuf[:, :],
                                                   pv[:, g, IV_TMV:IV_TMV + 1],
                                                   sh_ap, ALU.mult, ALU.add)
                            ii = v.scalar_tensor_tensor(xr_bf[:, g, :], dbuf[:, :],
                                                        pv[:, g, IV_TMR:IV_TMR + 1],
                                                        sh_ap, ALU.mult, ALU.add)
                        cx.mark("v", ii, f"mx_{j1}")
                        ii = v.tensor_copy(h_ext[:, :, 0:1],
                                           h_ext[:, :, TT1:TT1 + 1])
                        cx.mark("v", ii, f"hc_{j1 + 1}")


                if 0 <= i + 2 < NT1:
                    j0 = i + 2
                    tag0 = f"l0_{j0}"
                    tag1b = f"l1_{j0}"

                    @blk.vector
                    def _(v, i=i, tag1b=tag1b):
                        v_rows(v, tag1b, pb[1], pb[1], TT1, 0, xoff=0,
                               qoff=256, war_deps=[f"bc_l1_{i + 1}",
                                                   f"ir_l1_{i + 1}"])

                    @blk.scalar
                    def _(s, i=i, tag1b=tag1b):
                        s_irow(s, tag1b, TT1, 0, war_deps=[f"bc_l1_{i + 1}"])

                if 0 <= i + 3 < NT1:
                    j0 = i + 3
                    tag0c = f"l0_{j0}"

                    @blk.vector
                    def _(v, j0=j0, tag0c=tag0c):
                        v_rows(v, tag0c, pb[7], pb[7], TT1, 4, xoff=0,
                               qoff=256, war_deps=[f"bc_l0_{j0 - 1}",
                                                   f"ir_l0_{j0 - 1}"])

                    @blk.scalar
                    def _(s, j0=j0, tag0c=tag0c):
                        s_irow(s, tag0c, TT1, 4, war_deps=[f"bc_l0_{j0 - 1}"])


    with contextlib.ExitStack() as p2:
        wfk_s = p2.enter_context(nc.sbuf_tensor("wfk_s", [128, G, 4 * C], BF16))
        wfr_s = p2.enter_context(nc.sbuf_tensor("wfr_s", [128, G, C], BF16))
        wfvb = [p2.enter_context(nc.sbuf_tensor(f"wfvb{j}", [128, 4, C], BF16))
                for j in range(3)]
        x2in = [p2.enter_context(nc.sbuf_tensor(f"x2in{j}", [128, G, TT2 + 1], F32))
                for j in range(2)]
        rnd2 = p2.enter_context(nc.sbuf_tensor("rnd2", [128, G, TT2 + 2], F32R))
        sq2 = p2.enter_context(nc.sbuf_tensor("sq2", [128, G, TT2 + 2], F32R))
        h2_ext = p2.enter_context(nc.sbuf_tensor("h2_ext", [128, G, TT2 + 1], F32))
        d2 = p2.enter_context(nc.sbuf_tensor("d2", [128, TT2], F32))
        xk2 = p2.enter_context(nc.sbuf_tensor("xk2", [128, G, TT2], BF16))
        xr2 = p2.enter_context(nc.sbuf_tensor("xr2", [128, G, TT2], BF16))
        rwin = p2.enter_context(
            nc.sbuf_tensor("rwin", [128, G, TT2 + 1], F32))
        r3 = p2.enter_context(nc.sbuf_tensor("r3", [128, G, TT2], F32))
        kf = p2.enter_context(nc.sbuf_tensor("kf", [128, GH, TT2], BF16))
        relu_s = [p2.enter_context(nc.sbuf_tensor(f"relu{j}", [128, TT2], F32))
                  for j in range(2)]
        zpad = p2.enter_context(nc.sbuf_tensor("zpad", [128, G], F32))
        rpr = p2.enter_context(nc.sbuf_tensor("rpr", [128, G, TT2], F32))

        wfv_bv = wfv_b.rearrange("(cc f p) o -> p (cc f) o", p=128, f=4)

        with nc.Block() as blk:
            @blk.gpsimd
            def _(po):
                cx.dma_war("pool", po, "x2st")
                cx.dma_war("pool", po, "x1st")
                ii = po.memset(zpad[:, :], 0.0)
                cx.mark("pool", ii, "zp_a")
                ii = po.dma_start(out=wfk_s[:, :, :],
                                  in_=wfkT.rearrange("(kb p) o -> p kb o", p=128))
                cx.mark_dma("wfkld", ii, "ld_wfk")
                ii = po.dma_start(out=wfr_s[:, :, :],
                                  in_=wfrT.rearrange("(kb p) o -> p kb o", p=128))
                cx.mark_dma("wfrld", ii, "ld_wfr")

            @blk.vector
            def _(v):
                cx.need("v", v, "zp_a")
                v.tensor_copy(rnd2[:, :, TT2 + 1:TT2 + 2],
                              zpad[:, :].unsqueeze(-1))
                ii = v.tensor_copy(sq2[:, :, TT2 + 1:TT2 + 2],
                                   zpad[:, :].unsqueeze(-1))
                cx.mark("v", ii, "p2pad")

            for jj in range(-1, NT2):
                @blk.sync
                def _(sy, jj=jj):
                    if 0 <= jj + 1 < NT2:
                        j = jj + 1
                        if j <= 1:
                            cx.dma_war("sync", sy, "x2st")
                            cx.dma_war("sync", sy, "x1st")
                            cx.dma_war("sync", sy, "zc")
                        cx.need_if("sync", sy, f"rpr_{j - 2}")
                        cx.need_if("sync", sy, f"x2add_{j - 1}")
                        cx.dma_war("sync", sy, f"x2in{j % 2}")
                        cx.dma_war("sync", sy, "rwin")
                        if "hreg" not in hoff_reg:
                            hoff_reg["hreg"] = sy.alloc_register("hreg")
                            hoff_reg["oreg"] = sy.alloc_register("oreg")
                            pid = sy.partition_id()
                            hsv = sy.snap((pid % 2) * TO, min_val=0, max_val=TO)
                            sy.reg_mov(hoff_reg["hreg"], hsv)
                        sy.reg_add(hoff_reg["oreg"], hoff_reg["hreg"], j * TT2)
                        off = sy.snap(hoff_reg["oreg"], min_val=j * TT2,
                                      max_val=TO + j * TT2)
                        ii = sy.dma_start(out=x2in[j % 2][:, :, :],
                                          in_=x1s_v[:, :, bass.ds(off, TT2 + 1)])
                        cx.mark_dma(f"x2in{j % 2}", ii, f"ldx2_{j}")
                        ii = sy.dma_start(out=rwin[:, :, :],
                                          in_=rwkvs_v[:, :,
                                                      bass.ds(off, TT2 + 1)])
                        cx.mark_dma("rwin", ii, f"ldrw_{j}")
                    if 0 <= jj < NT2:
                        for ch in range(3):
                            cx.need_if("sync", sy,
                                       f"kvch_{jj - 1}_{[6, 7, 5][ch]}")
                            if ch == 0 and jj == 0:
                                cx.need("sync", sy, "cv_wfv")
                            cx.dma_war("sync", sy, f"wfv{ch % 3}")
                            ii = sy.dma_start(out=wfvb[ch % 3][:, :, :],
                                              in_=wfv_bv[:, 4 * ch:4 * ch + 4, :])
                            cx.mark_dma(f"wfv{ch % 3}", ii, f"wfv_{jj}_{ch}")

                if 0 <= jj + 1 < NT2:
                    j = jj + 1
                    tag2 = f"l2_{j}"

                    @blk.vector
                    def _(v, j=j):
                        cx.need("v", v, f"ldx2_{j}")
                        cx.need("v", v, f"ldrw_{j}")
                        cx.need_if("v", v, f"rpr_{j - 2}")
                        ii = v.tensor_tensor(x2in[j % 2][:, :, :],
                                             x2in[j % 2][:, :, :],
                                             rwin[:, :, :], ALU.add)
                        cx.mark("v", ii, f"x2add_{j}")

                    @blk.gpsimd
                    def _(po, j=j):
                        cx.need("pool", po, f"x2add_{j}")
                        cx.need_if("pool", po, f"st_l2_{j - 1}")
                        ii = po.tensor_copy(rnd2[:, :, 0:TT2 + 1],
                                            x2in[j % 2][:, :, :])
                        cx.mark("pool", ii, f"rd2_{j}")
                        ii = po.tensor_tensor(sq2[:, :, 0:TT2 + 1],
                                              x2in[j % 2][:, :, :],
                                              x2in[j % 2][:, :, :], ALU.mult)
                        cx.mark("pool", ii, f"sq2_{j}")

                def emit_ln2_tail(jj=jj):
                    if not (0 <= jj + 1 < NT2):
                        return
                    j = jj + 1
                    tag2 = f"l2_{j}"

                    @blk.tensor
                    def _(t, j=j, tag2=tag2, jj=jj):
                        pe_stats(t, tag2, rnd2[:, :, :], sq2[:, :, :], pb[0],
                                 pb[1], TT2 + 2,
                                 deps=[f"rd2_{j}", f"sq2_{j}", f"vr_l2_{j - 1}",
                                       f"kv_ev_{jj - 1}_1", "p2pad"])

                    @blk.vector
                    def _(v, j=j, tag2=tag2):
                        v_rows(v, tag2, pb[0], pb[1], TT2 + 1, 0,
                               war_deps=[f"bc_l2_{j - 1}", f"ir_l2_{j - 1}"])

                    @blk.scalar
                    def _(s, j=j, tag2=tag2):
                        s_irow(s, tag2, TT2 + 1, 0,
                               war_deps=[f"bc_l2_{j - 1}"])

                    @blk.tensor
                    def _(t, tag2=tag2, jj=jj):
                        pe_bcast(t, tag2, pb[2], pb[3], TT2 + 2, 0,
                                 war_deps=[f"ap_l2_{jj}", f"kv_ev_{jj - 1}_3"])

                    @blk.vector
                    def _(v, j=j, tag2=tag2):
                        v_apply_affine(v, tag2, pb[2], pb[3],
                                       x2in[j % 2][:, :, :], h2_ext[:, :, :],
                                       TT2 + 1, IV_LN2W, IV_LN2B,
                                       war_deps=[f"mx2_{j - 1}"])

                def emit_mx2(jj=jj):
                    if not (0 <= jj + 1 < NT2):
                        return

                    @blk.vector
                    def _(v, jj=jj):
                        j = jj + 1
                        cx.need("v", v, f"af_l2_{j}")
                        if j == 0:
                            v.tensor_scalar(h2_ext[:, :, 0:1],
                                            h2_ext[:, :, 0:1],
                                            hm[:, 0:1], None, ALU.mult)
                        cx.need_if("v", v, f"fk_{jj}_{GH - 1}")
                        cx.need_if("v", v, f"fr_{jj}_{G - 1}")
                        if DEBUG_TAPS:
                            cx.need_if("v", v, f"p2dbg_{jj}")
                        ii = None
                        for g in range(G):
                            sh_ap = h2_ext[:, g, 0:TT2]
                            v.tensor_tensor(d2[:, :], h2_ext[:, g, 1:TT2 + 1],
                                            sh_ap, ALU.subtract)
                            v.scalar_tensor_tensor(xk2[:, g, :], d2[:, :],
                                                   pv[:, g, IV_CMK:IV_CMK + 1],
                                                   sh_ap, ALU.mult, ALU.add)
                            ii = v.scalar_tensor_tensor(xr2[:, g, :], d2[:, :],
                                                        pv[:, g, IV_CMR:IV_CMR + 1],
                                                        sh_ap, ALU.mult, ALU.add)
                        cx.mark("v", ii, f"mx2_{j}")

                if not (0 <= jj < NT2):
                    emit_ln2_tail(jj)
                    emit_mx2(jj)

                if 0 <= jj < NT2:
                    for go in range(G):
                        @blk.tensor
                        def _(t, jj=jj, go=go):
                            if jj == 0 and go == 0:
                                cx.need("pe", t, "ld_wfr")
                            if go == 0:
                                cx.need("pe", t, f"mx2_{jj}")
                            bank = pb[4 + go % 2]
                            if go < 2:
                                cx.need_if("pe", t, f"kv_ev_{jj - 1}_{4 + go}")
                            else:
                                cx.need("pe", t, f"er2_{jj}_{go - 2}")
                            for kb in range(G):
                                mm = t.matmul(bank[:, 0:TT2],
                                              wfr_s[:, kb, go * 128:(go + 1) * 128],
                                              xr2[:, kb, :], start=(kb == 0),
                                              stop=(kb == G - 1))
                            cx.mark("pe", mm, f"fr_{jj}_{go}")

                        @blk.scalar
                        def _(s, jj=jj, go=go):
                            cx.need("s", s, f"fr_{jj}_{go}")
                            cx.need_if("s", s, f"kv_ev_{jj - 1}_{go}")
                            ii = s.activation(r3[:, go, :],
                                              pb[4 + go % 2][:, 0:TT2],
                                              AF.Exp, scale=-1.0)
                            cx.mark("s", ii, f"er2_{jj}_{go}")

                    @blk.vector
                    def _(v, jj=jj):
                        cx.need("v", v, f"er2_{jj}_{G - 1}")
                        ii = v.tensor_scalar(r3[:, :, :], r3[:, :, :], 1.0,
                                             None, ALU.add)
                        cx.mark("v", ii, f"r3a_{jj}")

                    @blk.scalar
                    def _(s, jj=jj):
                        cx.need("s", s, f"r3a_{jj}")
                        s.activation(r3[:, :, :], r3[:, :, :], AF.Ln)
                        ii = s.activation(r3[:, :, :], r3[:, :, :], AF.Exp,
                                          scale=-1.0)
                        cx.mark("s", ii, f"r3_{jj}")

                    for gh in range(GH):
                        if gh == 16:
                            emit_ln2_tail(jj)
                        @blk.tensor
                        def _(t, jj=jj, gh=gh):
                            if jj == 0 and gh == 0:
                                cx.need("pe", t, "ld_wfk")
                            bank = pb[6 + gh % 2]
                            if gh < 2:
                                cx.need_if("pe", t, f"kv_ev_{jj - 1}_{6 + gh}")
                            else:
                                cx.need("pe", t, f"fkm_{jj}_{gh - 2}")
                            for kb in range(G):
                                mm = t.matmul(bank[:, 0:TT2],
                                              wfk_s[:, kb, gh * 128:(gh + 1) * 128],
                                              xk2[:, kb, :], start=(kb == 0),
                                              stop=(kb == G - 1))
                            cx.mark("pe", mm, f"fk_{jj}_{gh}")

                        @blk.scalar
                        def _(s, jj=jj, gh=gh):
                            cx.need("s", s, f"fk_{jj}_{gh}")
                            cx.need_if("s", s, f"fkm_{jj}_{gh - 2}")
                            ii = s.activation(relu_s[gh % 2][:, :],
                                              pb[6 + gh % 2][:, 0:TT2], AF.Relu)
                            cx.mark("s", ii, f"rl_{jj}_{gh}")

                        @blk.vector
                        def _(v, jj=jj, gh=gh):
                            cx.need("v", v, f"rl_{jj}_{gh}")
                            cx.need_if("v", v, f"kvl_{jj - 1}")
                            if DEBUG_TAPS:
                                cx.need_if("v", v, f"p2dbg_{jj - 1}")
                            ii = v.tensor_tensor(kf[:, gh, :], relu_s[gh % 2][:, :],
                                                 pb[6 + gh % 2][:, 0:TT2], ALU.mult)
                            cx.mark("v", ii, f"fkm_{jj}_{gh}")

                    for ch in range(8):
                        @blk.tensor
                        def _(t, jj=jj, ch=ch):
                            if ch == 0:
                                cx.need_if("pe", t, f"vr_l2_{jj + 1}")
                                cx.need_if("pe", t, f"ap_l2_{jj + 1}")
                                cx.need("pe", t, f"er2_{jj}_{G - 1}")
                                cx.need("pe", t, f"fkm_{jj}_{GH - 1}")
                            cx.need("pe", t, f"wfv_{jj}_{ch}")
                            for sl in range(4):
                                hb = ch * 4 + sl
                                for go in range(G):
                                    mm = t.matmul(pb[go][:, 0:TT2],
                                                  wfvb[ch % 3][:, sl,
                                                               go * 128:(go + 1) * 128],
                                                  kf[:, hb, :], start=(hb == 0),
                                                  stop=(hb == GH - 1))
                                    if hb == GH - 1:
                                        cx.mark("pe", mm, f"kvf_{jj}_{go}")
                            if ch != 7:
                                cx.mark("pe", mm, f"kvch_{jj}_{ch}")
                            else:
                                cx.ev[f"kvch_{jj}_7"] = ("pe", cx.cnt["pe"])
                                cx.ev[f"kvl_{jj}"] = ("pe", cx.cnt["pe"])

                        if ch < 5:
                            @blk.sync
                            def _(sy, jj=jj, ch=ch):
                                cx.need("sync", sy, f"kvch_{jj}_{ch}")
                                cx.dma_war("sync", sy, f"wfv{ch % 3}")
                                ii = sy.dma_start(
                                    out=wfvb[ch % 3][:, :, :],
                                    in_=wfv_bv[:, 4 * (ch + 3):4 * (ch + 3) + 4, :])
                                cx.mark_dma(f"wfv{ch % 3}", ii,
                                            f"wfv_{jj}_{ch + 3}")

                    emit_mx2(jj)

                    @blk.vector
                    def _(v, jj=jj):
                        cx.need("v", v, f"r3_{jj}")
                        cx.need_if("v", v, f"rst_{jj - 1}")
                        ii = None
                        for go in (4, 5, 6, 7, 0, 1, 2, 3):
                            cx.need("v", v, f"kvf_{jj}_{go}")
                            v.tensor_tensor(rpr[:, go, :], r3[:, go, :],
                                            pb[go][:, 0:TT2], ALU.mult)
                            ii = v.tensor_tensor(rpr[:, go, :], rpr[:, go, :],
                                                 x2in[jj % 2][:, go, 1:TT2 + 1],
                                                 ALU.add)
                            cx.mark("v", ii, f"kv_ev_{jj}_{go}")
                        cx.ev[f"rpr_{jj}"] = ("v", cx.cnt["v"])

                    @blk.scalar
                    def _(s, jj=jj):
                        cx.need("s", s, f"rpr_{jj}")
                        ii = s.dma_start(out=rs_v[:, :, jj * TT2:(jj + 1) * TT2],
                                         in_=rpr[:, :, :])
                        cx.mark_dma("rst", ii, f"rst_{jj}")
                        if DEBUG_TAPS:
                            ii = s.dma_start(
                                out=kfdbg_v[:, :, jj * TT2:(jj + 1) * TT2],
                                in_=kf[:, :, :])
                            cx.mark_dma("ydbg", ii)

                if DEBUG_TAPS and 0 <= jj < NT2:
                    @blk.scalar
                    def _(s, jj=jj):
                        cx.need("s", s, f"fk_{jj}_{GH - 1}")
                        ii = s.dma_start(
                            out=k2dbg_v[:, :, jj * TT2:(jj + 1) * TT2],
                            in_=xk2[:, :, :])
                        cx.mark_dma("ydbg", ii, f"p2dbg_{jj}")


    with contextlib.ExitStack() as p3:
        w1_s = p3.enter_context(nc.sbuf_tensor("w1_s", [128, G, 4 * C], BF16))
        w2b_s = [p3.enter_context(nc.sbuf_tensor(f"w2b{j}", [128, 4, C], BF16))
                 for j in range(3)]
        rin = [p3.enter_context(nc.sbuf_tensor(f"rin{j}", [128, G, TT3], F32))
               for j in range(2)]
        x0in = p3.enter_context(nc.sbuf_tensor("x0in", [128, G, TT3], F32))
        rnd3 = p3.enter_context(nc.sbuf_tensor("rnd3", [128, G, TT3], F32R))
        sq3 = p3.enter_context(nc.sbuf_tensor("sq3", [128, G, TT3], F32R))
        rnd4 = rnd3
        sq4 = sq3
        x3 = [p3.enter_context(nc.sbuf_tensor(f"x3_{j}", [128, G, TT3], F32))
              for j in range(2)]
        x3b = [p3.enter_context(nc.sbuf_tensor(f"x3b{j}", [128, G, TT3], BF16))
               for j in range(2)]
        hf = p3.enter_context(nc.sbuf_tensor("hf", [128, GH, TT3], BF16))
        x4p = [p3.enter_context(nc.sbuf_tensor(f"x4p{j}", [128, G, TT3], F32))
               for j in range(2)]
        outb = p3.enter_context(nc.sbuf_tensor("outb", [128, G, TT3], F32))

        w2_bv = w2_b.rearrange("(cc f p) o -> p (cc f) o", p=128, f=4)

        with nc.Block() as blk:
            @blk.gpsimd
            def _(po):
                cx.need("pool", po, f"rst_{NT2 - 1}")
                ii = po.dma_start(out=w1_s[:, :, :],
                                  in_=w1.rearrange("(kb p) o -> p kb o", p=128))
                cx.mark_dma("w1ld", ii, "ld_w1")

            for jj in range(-1, NT3 + 2):
                @blk.sync
                def _(sy, jj=jj):
                    if 0 <= jj + 1 < NT3:
                        j = jj + 1
                        if j <= 1:
                            cx.need("sync", sy, f"rst_{NT2 - 1}")
                        cx.need_if("sync", sy, f"x4ev_{j - 2}_{G - 1}")
                        cx.need_if("sync", sy, f"x3pre_{j - 1}")
                        cx.dma_war("sync", sy, f"rin{j % 2}")
                        cx.dma_war("sync", sy, "xo")
                        ii = sy.dma_start(out=rin[j % 2][:, :, :],
                                          in_=rs_v[:, :, j * TT3:(j + 1) * TT3])
                        cx.mark_dma(f"rin{j % 2}", ii, f"ldr_{j}")
                        ii = sy.dma_start(out=x0in[:, :, :],
                                          in_=xTo_v[:, :, j * TT3:(j + 1) * TT3])
                        cx.mark_dma("xo", ii, f"ldo_{j}")
                    if 0 <= jj < NT3:
                        for ch in range(3):
                            cx.need_if("sync", sy,
                                       f"w2ch_{jj - 1}_{[6, 7, 5][ch]}")
                            if ch == 0 and jj == 0:
                                cx.need("sync", sy, "cv_w2")
                            cx.dma_war("sync", sy, f"w2s{ch % 3}")
                            ii = sy.dma_start(out=w2b_s[ch % 3][:, :, :],
                                              in_=w2_bv[:, 4 * ch:4 * ch + 4, :])
                            cx.mark_dma(f"w2s{ch % 3}", ii, f"w2s_{jj}_{ch}")

                if 0 <= jj - 2 < NT3:
                    j2 = jj - 2
                    tagB = f"lB_{j2}"

                    @blk.tensor
                    def _(t, j2=j2, tagB=tagB, jj=jj):
                        pe_stats(t, tagB, rnd4[:, :, :], sq4[:, :, :], pb[2],
                                 pb[3], TT3,
                                 deps=[f"rd4_{j2}", f"sq4_{j2}",
                                       f"x4ev_{jj - 1}_3",
                                       f"vr_lB_{j2 - 1}", f"ap_lA_{jj}"])

                    @blk.vector
                    def _(v, tagB=tagB):
                        v_rows(v, tagB, pb[2], pb[3], TT3, 4)

                    @blk.scalar
                    def _(s, tagB=tagB):
                        s_irow(s, tagB, TT3, 4)

                    @blk.tensor
                    def _(t, tagB=tagB):
                        pe_bcast(t, tagB, pb[2], pb[3], TT3, 4,
                                 war_deps=[f"vr_{tagB}"])

                    @blk.vector
                    def _(v, j2=j2, tagB=tagB):
                        v_apply(v, tagB, pb[2], pb[3], x4p[j2 % 2][:, :, :],
                                outb[:, :, :], TT3, war_deps=[f"ost_{j2 - 1}"])

                    @blk.scalar
                    def _(s, j2=j2, tagB=tagB):
                        cx.need("s", s, f"ap_{tagB}")
                        ii = s.dma_start(
                            out=outT_v[:, :, j2 * TT3:(j2 + 1) * TT3],
                            in_=outb[:, :, :])
                        cx.mark_dma("ost", ii, f"ost_{j2}")

                if 0 <= jj + 1 < NT3:
                    j = jj + 1

                    @blk.gpsimd
                    def _(po, j=j):
                        cx.need("pool", po, f"ldr_{j}")
                        cx.need("pool", po, f"ldo_{j}")
                        ii = po.tensor_tensor(rin[j % 2][:, :, :],
                                              rin[j % 2][:, :, :],
                                              x0in[:, :, :], ALU.add)
                        cx.mark("pool", ii, f"x3pre_{j}")
                        cx.need_if("pool", po, f"st_lB_{j - 3}")
                        ii = po.tensor_copy(rnd3[:, :, :], rin[j % 2][:, :, :])
                        cx.mark("pool", ii, f"rd3_{j}")
                        ii = po.tensor_tensor(sq3[:, :, :], rin[j % 2][:, :, :],
                                              rin[j % 2][:, :, :], ALU.mult)
                        cx.mark("pool", ii, f"sq3_{j}")

                def emit_lnA_tail(jj=jj):
                    if not (0 <= jj + 1 < NT3):
                        return
                    j = jj + 1
                    tagA = f"lA_{j}"

                    @blk.tensor
                    def _(t, j=j, tagA=tagA, jj=jj):
                        pe_stats(t, tagA, rnd3[:, :, :], sq3[:, :, :], pb[0],
                                 pb[1], TT3,
                                 deps=[f"rd3_{j}", f"sq3_{j}", f"vr_lA_{j - 1}",
                                       f"x4ev_{jj - 1}_1"])

                    @blk.vector
                    def _(v, j=j, tagA=tagA):
                        v_rows(v, tagA, pb[0], pb[1], TT3, 0,
                               war_deps=[f"bc_lA_{j - 1}", f"ir_lA_{j - 1}"])

                    @blk.scalar
                    def _(s, j=j, tagA=tagA):
                        s_irow(s, tagA, TT3, 0, war_deps=[f"bc_lA_{j - 1}"])

                    @blk.tensor
                    def _(t, tagA=tagA, jj=jj):
                        pe_bcast(t, tagA, pb[2], pb[3], TT3, 0,
                                 war_deps=[f"ap_lB_{jj - 2}", f"x4ev_{jj - 1}_3"])

                    @blk.vector
                    def _(v, j=j, tagA=tagA):
                        v_apply(v, tagA, pb[2], pb[3], rin[j % 2][:, :, :],
                                x3[j % 2][:, :, :], TT3,
                                war_deps=[f"x4ev_{j - 2}_{G - 1}"])

                    @blk.vector
                    def _(v, j=j, tagA=tagA):
                        cx.need_if("v", v, f"w1l_{j - 2}")
                        ii = v.tensor_copy(x3b[j % 2][:, :, :],
                                           x3[j % 2][:, :, :])
                        cx.mark("v", ii, f"x3b_{j}")

                if not (0 <= jj < NT3):
                    emit_lnA_tail(jj)

                if 0 <= jj < NT3:
                    for gh in range(GH):
                        if gh == 8:
                            emit_lnA_tail(jj)
                        @blk.tensor
                        def _(t, jj=jj, gh=gh):
                            if jj == 0 and gh == 0:
                                cx.need("pe", t, "ld_w1")
                            if gh == 0:
                                cx.need("pe", t, f"x3b_{jj}")
                            bank = pb[4 + gh % 2]
                            if gh < 2:
                                cx.need_if("pe", t, f"x4ev_{jj - 1}_{4 + gh}")
                            else:
                                cx.need("pe", t, f"hf_{jj}_{gh - 2}")
                            for kb in range(G):
                                mm = t.matmul(bank[:, 0:TT3],
                                              w1_s[:, kb, gh * 128:(gh + 1) * 128],
                                              x3b[jj % 2][:, kb, :],
                                              start=(kb == 0), stop=(kb == G - 1))
                            mk = f"w1l_{jj}" if gh == GH - 1 else None
                            cx.mark("pe", mm, f"w1_{jj}_{gh}")
                            if mk:
                                cx.ev[mk] = ("pe", cx.cnt["pe"])

                        @blk.scalar
                        def _(s, jj=jj, gh=gh):
                            cx.need("s", s, f"w1_{jj}_{gh}")
                            cx.need_if("s", s, f"w2hb_{jj - 1}_{gh}")
                            ii = s.activation(hf[:, gh, :],
                                              pb[4 + gh % 2][:, 0:TT3],
                                              AF.Relu, bias=pvH[:, gh:gh + 1])
                            cx.mark("s", ii, f"hf_{jj}_{gh}")

                    for ch in range(8):
                        @blk.tensor
                        def _(t, jj=jj, ch=ch):
                            if ch == 0:
                                cx.need_if("pe", t, f"vr_lA_{jj + 1}")
                                cx.need_if("pe", t, f"ap_lA_{jj + 1}")
                                cx.need("pe", t, f"hf_{jj}_{GH - 1}")
                                cx.need_if("pe", t, f"x4ev_{jj - 1}_{G - 1}")
                            cx.need("pe", t, f"w2s_{jj}_{ch}")
                            for sl in range(4):
                                hb = ch * 4 + sl
                                for go in range(G):
                                    mm = t.matmul(pb[go][:, 0:TT3],
                                                  w2b_s[ch % 3][:, sl,
                                                                go * 128:(go + 1) * 128],
                                                  hf[:, hb, :], start=(hb == 0),
                                                  stop=(hb == GH - 1))
                                    if hb == GH - 1:
                                        cx.mark("pe", mm, f"w2o_{jj}_{go}")
                            if ch != 7:
                                cx.mark("pe", mm, f"w2ch_{jj}_{ch}")
                            else:
                                cx.ev[f"w2ch_{jj}_7"] = ("pe", cx.cnt["pe"])
                        if ch < 5:
                            @blk.sync
                            def _(sy, jj=jj, ch=ch):
                                cx.need("sync", sy, f"w2ch_{jj}_{ch}")
                                cx.dma_war("sync", sy, f"w2s{ch % 3}")
                                ii = sy.dma_start(
                                    out=w2b_s[ch % 3][:, :, :],
                                    in_=w2_bv[:, 4 * (ch + 3):4 * (ch + 3) + 4, :])
                                cx.mark_dma(f"w2s{ch % 3}", ii,
                                            f"w2s_{jj}_{ch + 3}")

                    @blk.tensor
                    def _(t, jj=jj):
                        cx.ev[f"w2hb_{jj}_0"] = ("pe", cx.cnt["pe"])
                        for gh in range(GH):
                            cx.ev[f"w2hb_{jj}_{gh}"] = ("pe", cx.cnt["pe"])

                    @blk.vector
                    def _(v, jj=jj):
                        cx.need_if("v", v, f"ost_{jj - 2}")
                        ii = None
                        for go in range(G):
                            cx.need("v", v, f"w2o_{jj}_{go}")
                            ii = v.scalar_tensor_tensor(
                                x4p[jj % 2][:, go, :], pb[go][:, 0:TT3],
                                pv[:, go, IV_B2:IV_B2 + 1],
                                x3[jj % 2][:, go, :], ALU.add, ALU.add)
                            cx.mark("v", ii, f"x4ev_{jj}_{go}")
                        cx.ev[f"x4_{jj}"] = ("v", cx.cnt["v"])

                if 0 <= jj - 1 < NT3:
                    j = jj - 1
                    tagB = f"lB_{j}"

                    @blk.vector
                    def _(v, j=j):
                        cx.need("v", v, f"x4_{j}")
                        cx.need_if("v", v, f"st_lA_{j + 2}")
                        ii = v.tensor_copy(rnd4[:, :, :], x4p[j % 2][:, :, :])
                        cx.mark("v", ii, f"rd4_{j}")

                    @blk.scalar
                    def _(s, j=j):
                        cx.need("s", s, f"x4_{j}")
                        cx.need_if("s", s, f"st_lA_{j + 2}")
                        ii = s.activation(sq4[:, :, :], x4p[j % 2][:, :, :],
                                          AF.Square)
                        cx.mark("s", ii, f"sq4_{j}")


        with nc.Block() as blkf:
            @blkf.gpsimd
            def _(po):
                prod, val = cx.ev[f"ost_{NT3 - 1}"]
                po.wait_ge(cx.sem[prod], val)

    cx.es.close()
    return nc


def kernel(**inputs):
    x = np.ascontiguousarray(np.asarray(inputs["L0_attn_x"], np.float32))

    def t_(a):
        return np.ascontiguousarray(np.asarray(a, np.float32).T)

    wk_t = t_(inputs["L0_attn_Wk"])
    wv_t = t_(inputs["L0_attn_Wv"])
    wr_t = t_(inputs["L0_attn_Wr"])
    wo_t = t_(inputs["L0_attn_Wo"])
    wfk_t = t_(inputs["L0_attn_Wfk"])
    wfv_t = t_(inputs["L0_attn_Wfv"])
    wfr_t = t_(inputs["L0_attn_Wfr"])
    w1_a = np.ascontiguousarray(np.asarray(inputs["L0_ffn_W1"], np.float32))
    w2_a = np.ascontiguousarray(np.asarray(inputs["L0_ffn_W2"], np.float32))

    def pv_pack(names):
        arr = np.zeros((128, G, len(names)), np.float32)
        for i, nm in enumerate(names):
            v = np.asarray(inputs[nm], np.float32).reshape(G, 128)
            arr[:, :, i] = v.T
        return np.ascontiguousarray(arr)

    pvec = pv_pack(["L0_attn_ln0_w", "L0_attn_ln0_b", "L0_attn_ln1_w",
                    "L0_attn_ln1_b", "L0_attn_ln2_w", "L0_attn_ln2_b",
                    "L0_attn_time_decay", "L0_attn_time_first", "L0_attn_tmk",
                    "L0_attn_tmv", "L0_attn_tmr", "L0_attn_cmk",
                    "L0_attn_cmr", "L0_ffn_b2"])
    b1 = np.asarray(inputs["L0_ffn_b1"], np.float32).reshape(GH, 128)
    pvecH = np.ascontiguousarray(b1.T)

    xT_all = [np.ascontiguousarray(x[b].T) for b in range(B)]

    in_maps = []
    for core in range(8):
        b, h = core // 2, core % 2
        in_maps.append({
            "xT": xT_all[b],
            "xTo": np.ascontiguousarray(xT_all[b][:, h * TO:(h + 1) * TO]),
            "wkT": wk_t, "wvT": wv_t, "wrT": wr_t, "woT": wo_t,
            "wfkT": wfk_t, "wfvT": wfv_t, "wfrT": wfr_t,
            "w1": w1_a, "w2": w2_a,
            "pvec": pvec, "pvecH": pvecH,
            "hmask": np.full((128, 1), float(h), np.float32),
        })

    nc = build_program()
    res = run_bass_kernel_spmd(nc, in_maps, core_ids=list(range(8)))

    out = np.empty((B, T, C), np.float32)
    for core in range(8):
        b, h = core // 2, core % 2
        out[b, h * TO:(h + 1) * TO, :] = np.asarray(
            res.results[core]["outT"], np.float32).T
    return out


if __name__ == "__main__":
    nc = build_program()
    print("build ok; instructions:",
          sum(1 for _ in nc.all_instructions()) if hasattr(nc, "all_instructions") else "?")


# version 46
# speedup vs baseline: 1.0797x; 1.0797x over previous
import contextlib
import os
import numpy as np
import concourse.bass as bass
import concourse.mybir as mybir
from concourse.bass_utils import run_bass_kernel_spmd

F32 = mybir.dt.float32
F32R = mybir.dt.float32r
BF16 = mybir.dt.bfloat16
I32 = mybir.dt.int32
AF = mybir.ActivationFunctionType
ALU = mybir.AluOpType

C = 1024
G = 8
GH = 32
T = 8192
TO = 4096
B = 4
EPS = 1e-5

TT1 = 256
NT1 = T // TT1
TT2 = 256
NT2 = TO // TT2
TT3 = 256
NT3 = TO // TT3

IV_LN0W, IV_LN0B, IV_LN1W, IV_LN1B, IV_LN2W, IV_LN2B = 0, 1, 2, 3, 4, 5
IV_TD, IV_TF, IV_TMK, IV_TMV, IV_TMR, IV_CMK, IV_CMR, IV_B2 = 6, 7, 8, 9, 10, 11, 12, 13
NV = 14
DEBUG_TAPS = False


class Ctx:
    ENGS = ("pe", "v", "s", "pool", "sync")

    def __init__(self, nc):
        self.nc = nc
        self.es = contextlib.ExitStack()
        self.sem = {e: self.es.enter_context(nc.semaphore(f"sem_{e}"))
                    for e in self.ENGS}
        self.cnt = {e: 0 for e in self.ENGS}
        self.ev = {}
        self.wm = {}

    def mark(self, eng, instr, label=None, dma=False):
        assert not dma
        self.cnt[eng] += 1
        instr.then_inc(self.sem[eng], 1)
        if label is not None:
            self.ev[label] = (eng, self.cnt[eng])
        return self.cnt[eng]

    def mark_dma(self, stream, instr, label=None):
        key = "dma:" + stream
        if key not in self.sem:
            self.sem[key] = self.es.enter_context(
                self.nc.semaphore(f"sem_{stream}"))
            self.cnt[key] = 0
        self.cnt[key] += 16
        instr.then_inc(self.sem[key], 16)
        if label is not None:
            self.ev[label] = (key, self.cnt[key])
        return self.cnt[key]

    def need(self, consumer, engine_obj, label):
        prod, val = self.ev[label]
        if prod == consumer:
            return
        key = (consumer, prod)
        if self.wm.get(key, -1) >= val:
            return
        engine_obj.wait_ge(self.sem[prod], val)
        self.wm[key] = val

    def need_if(self, consumer, engine_obj, label):
        if label in self.ev:
            self.need(consumer, engine_obj, label)

    def dma_war(self, consumer, engine_obj, stream):
        key = "dma:" + stream
        if key not in self.cnt or self.cnt[key] == 0:
            return
        val = self.cnt[key]
        wkey = (consumer, key)
        if self.wm.get(wkey, -1) >= val:
            return
        engine_obj.wait_ge(self.sem[key], val)
        self.wm[wkey] = val


def build_program():
    nc = bass.Bass("TRN2", target_bir_lowering=False)

    xT = nc.dram_tensor("xT", [C, T], F32, kind="ExternalInput").ap()
    xTo = nc.dram_tensor("xTo", [C, TO], F32, kind="ExternalInput").ap()
    wkT = nc.dram_tensor("wkT", [C, C], F32, kind="ExternalInput").ap()
    wvT = nc.dram_tensor("wvT", [C, C], F32, kind="ExternalInput").ap()
    wrT = nc.dram_tensor("wrT", [C, C], F32, kind="ExternalInput").ap()
    woT = nc.dram_tensor("woT", [C, C], F32, kind="ExternalInput").ap()
    wfkT = nc.dram_tensor("wfkT", [C, 4 * C], F32, kind="ExternalInput").ap()
    wfvT = nc.dram_tensor("wfvT", [4 * C, C], F32, kind="ExternalInput").ap()
    wfrT = nc.dram_tensor("wfrT", [C, C], F32, kind="ExternalInput").ap()
    w1 = nc.dram_tensor("w1", [C, 4 * C], F32, kind="ExternalInput").ap()
    w2 = nc.dram_tensor("w2", [4 * C, C], F32, kind="ExternalInput").ap()
    pvec = nc.dram_tensor("pvec", [128, G, NV], F32, kind="ExternalInput").ap()
    pvecH = nc.dram_tensor("pvecH", [128, GH], F32, kind="ExternalInput").ap()
    hmask = nc.dram_tensor("hmask", [128, 1], F32, kind="ExternalInput").ap()
    outT = nc.dram_tensor("outT", [C, TO], F32, kind="ExternalOutput").ap()

    rwkvs = nc.dram_tensor("rwkvs", [C, T + 1], F32).ap()
    x1s = nc.dram_tensor("x1s", [C, T + 1], F32).ap()
    ydbg = nc.dram_tensor("ydbg", [C, T], BF16).ap() if DEBUG_TAPS else None
    ydbg_v = ydbg.rearrange("(g p) t -> p g t", p=128) if DEBUG_TAPS else None
    kdbg = nc.dram_tensor("kdbg", [C, T], BF16).ap() if DEBUG_TAPS else None
    kdbg_v = kdbg.rearrange("(g p) t -> p g t", p=128) if DEBUG_TAPS else None
    vdbg = nc.dram_tensor("vdbg", [C, T], BF16).ap() if DEBUG_TAPS else None
    vdbg_v = vdbg.rearrange("(g p) t -> p g t", p=128) if DEBUG_TAPS else None
    x1dbg = nc.dram_tensor("x1dbg", [C, T], F32R).ap() if DEBUG_TAPS else None
    x1dbg_v = x1dbg.rearrange("(g p) t -> p g t", p=128) if DEBUG_TAPS else None
    k2dbg = nc.dram_tensor("k2dbg", [C, TO], BF16).ap() if DEBUG_TAPS else None
    k2dbg_v = k2dbg.rearrange("(g p) t -> p g t", p=128) if DEBUG_TAPS else None
    kfdbg = nc.dram_tensor("kfdbg", [4 * C, TO], BF16).ap() if DEBUG_TAPS else None
    kfdbg_v = kfdbg.rearrange("(g p) t -> p g t", p=128) if DEBUG_TAPS else None
    rs = nc.dram_tensor("rs", [C, TO], F32).ap()
    wfv_b = nc.dram_tensor("wfv_b", [4 * C, C], BF16).ap()
    w2_b = nc.dram_tensor("w2_b", [4 * C, C], BF16).ap()

    rwkvs_v = rwkvs.rearrange("(g p) t -> p g t", p=128)
    x1s_v = x1s.rearrange("(g p) t -> p g t", p=128)
    rs_v = rs.rearrange("(g p) t -> p g t", p=128)
    outT_v = outT.rearrange("(g p) t -> p g t", p=128)
    xT_v = xT.rearrange("(g p) t -> p g t", p=128)
    xTo_v = xTo.rearrange("(g p) t -> p g t", p=128)

    cx = Ctx(nc)
    es = cx.es

    pv = es.enter_context(nc.sbuf_tensor("pv", [128, G, NV], F32))
    pvH = es.enter_context(nc.sbuf_tensor("pvH", [128, GH], F32))
    hm = es.enter_context(nc.sbuf_tensor("hm", [128, 1], F32))
    eu8 = es.enter_context(nc.sbuf_tensor("eu8", [128, G], F32))
    dec8 = es.enter_context(nc.sbuf_tensor("dec8", [128, G], F32))
    onesr = es.enter_context(nc.sbuf_tensor("onesr", [128, 1], F32R))
    onesf = es.enter_context(nc.sbuf_tensor("onesf", [128, 1], F32))
    onesrow = es.enter_context(nc.sbuf_tensor("onesrow", [1, 128], F32R))
    onesrowf = es.enter_context(nc.sbuf_tensor("onesrowf", [1, 128], F32))
    epsb = es.enter_context(nc.sbuf_tensor("epsb", [128, 1], F32))
    ROWN = 260
    rows = es.enter_context(nc.sbuf_tensor("rows", [1, 8 * ROWN], F32R))

    def row(idx):
        return rows[0:1, idx * ROWN:idx * ROWN + ROWN]

    pb = [es.enter_context(nc.psum_tensor(f"pb{i}", [128, 512], F32))
          for i in range(8)]

    hoff_reg = {}

    def pe_stats(t_eng, tag, rnd_ap, sq_ap, bank_x, bank_q, n, deps,
                 xoff=0, qoff=0):
        cx.need_if("pe", t_eng, "consts")
        for lbl in deps:
            cx.need_if("pe", t_eng, lbl)
        for kb in range(G):
            mm = t_eng.matmul(bank_x[0:1, xoff:xoff + n], onesr[:, 0:1],
                              rnd_ap[:, kb, :],
                              start=(kb == 0), stop=(kb == G - 1))
        cx.mark("pe", mm, f"stx_{tag}")
        for kb in range(G):
            mm = t_eng.matmul(bank_q[0:1, qoff:qoff + n], onesr[:, 0:1],
                              sq_ap[:, kb, :],
                              start=(kb == 0), stop=(kb == G - 1))
        cx.mark("pe", mm, f"st_{tag}")

    def v_rows(v_eng, tag, bank_x, bank_q, n, rbase, xoff=0, qoff=0,
               war_deps=()):
        cx.need("v", v_eng, f"st_{tag}")
        for lbl in war_deps:
            cx.need_if("v", v_eng, lbl)
        m_r, m2_r, ve_r = row(rbase), row(rbase + 1), row(rbase + 2)
        v_eng.tensor_scalar(m_r[0:1, 0:n], bank_x[0:1, xoff:xoff + n],
                            1.0 / C, None, ALU.mult)
        v_eng.tensor_tensor(m2_r[0:1, 0:n], m_r[0:1, 0:n], m_r[0:1, 0:n],
                            ALU.mult)
        ii = v_eng.scalar_tensor_tensor(ve_r[0:1, 0:n],
                                        bank_q[0:1, qoff:qoff + n],
                                        1.0 / C, m2_r[0:1, 0:n],
                                        ALU.mult, ALU.subtract)
        cx.mark("v", ii, f"vr_{tag}")

    def s_irow(s_eng, tag, n, rbase, war_deps=()):
        cx.need("s", s_eng, f"vr_{tag}")
        for lbl in war_deps:
            cx.need_if("s", s_eng, lbl)
        ve_r, i_r = row(rbase + 2), row(rbase + 3)
        s_eng.activation(i_r[0:1, 0:n], ve_r[0:1, 0:n], AF.Ln,
                         bias=epsb[0:1, 0:1])
        ii = s_eng.activation(i_r[0:1, 0:n], i_r[0:1, 0:n], AF.Exp, scale=-0.5)
        cx.mark("s", ii, f"ir_{tag}")

    def pe_bcast(t_eng, tag, bank_m, bank_i, n, rbase, war_deps=(),
                 moff=0, ioff=0):
        cx.need("pe", t_eng, f"vr_{tag}")
        cx.need("pe", t_eng, f"ir_{tag}")
        cx.need_if("pe", t_eng, "rows0")
        for lbl in war_deps:
            cx.need_if("pe", t_eng, lbl)
        t_eng.matmul(bank_m[:, moff:moff + n], onesrow[0:1, 0:128],
                     row(rbase)[0:1, 0:n], start=True, stop=True)
        ii = t_eng.matmul(bank_i[:, ioff:ioff + n], onesrow[0:1, 0:128],
                          row(rbase + 3)[0:1, 0:n], start=True, stop=True)
        cx.mark("pe", ii, f"bc_{tag}")

    def v_apply(v_eng, tag, bank_m, bank_i, x_ap, out_ap, n, war_deps=(),
                moff=0, ioff=0):
        cx.need("v", v_eng, f"bc_{tag}")
        for lbl in war_deps:
            cx.need_if("v", v_eng, lbl)
        mb = bank_m[:, moff:moff + n].unsqueeze(1).broadcast_to([128, G, n])
        ib = bank_i[:, ioff:ioff + n].unsqueeze(1).broadcast_to([128, G, n])
        v_eng.scalar_tensor_tensor(out_ap, mb, -1.0, x_ap, ALU.mult, ALU.add)
        ii = v_eng.tensor_tensor(out_ap, out_ap, ib, ALU.mult)
        cx.mark("v", ii, f"ap_{tag}")

    def v_apply_affine(v_eng, tag, bank_m, bank_i, x_ap, out_ap, n, iv_w,
                       iv_b, war_deps=(), moff=0, ioff=0):
        cx.need("v", v_eng, f"bc_{tag}")
        for lbl in war_deps:
            cx.need_if("v", v_eng, lbl)
        mb = bank_m[:, moff:moff + n].unsqueeze(1).broadcast_to([128, G, n])
        ib = bank_i[:, ioff:ioff + n].unsqueeze(1).broadcast_to([128, G, n])
        v_eng.scalar_tensor_tensor(out_ap, mb, -1.0, x_ap, ALU.mult, ALU.add)
        ii = v_eng.tensor_tensor(out_ap, out_ap, ib, ALU.mult)
        cx.mark("v", ii, f"ap_{tag}")
        for g in range(G):
            ii = v_eng.tensor_scalar(out_ap[:, g, :], out_ap[:, g, :],
                                     pv[:, g, iv_w:iv_w + 1],
                                     pv[:, g, iv_b:iv_b + 1], ALU.mult, ALU.add)
        cx.mark("v", ii, f"af_{tag}")

    def s_affine(s_eng, tag, z_ap, out_ap, iv_w, iv_b, war_deps=()):
        cx.need("s", s_eng, f"ap_{tag}")
        for lbl in war_deps:
            cx.need_if("s", s_eng, lbl)
        for g in range(G):
            ii = s_eng.activation(out_ap[:, g, :], z_ap[:, g, :], AF.Identity,
                                  bias=pv[:, g, iv_b:iv_b + 1],
                                  scale=pv[:, g, iv_w:iv_w + 1])
        cx.mark("s", ii, f"af_{tag}")

    with contextlib.ExitStack() as p1:
        wk_s = p1.enter_context(nc.sbuf_tensor("wk_s", [128, G, C], BF16))
        wv_s = p1.enter_context(nc.sbuf_tensor("wv_s", [128, G, C], BF16))
        wr_s = p1.enter_context(nc.sbuf_tensor("wr_s", [128, G, C], BF16))
        wo_s = p1.enter_context(nc.sbuf_tensor("wo_s", [128, G, C], BF16))
        xin = [p1.enter_context(nc.sbuf_tensor(f"xin{j}", [128, G, TT1], F32R))
               for j in range(2)]
        sqb = p1.enter_context(nc.sbuf_tensor("sqb", [128, G, TT1], F32R))
        sqb2 = p1.enter_context(nc.sbuf_tensor("sqb2", [128, G, TT1], F32R))
        x1r = [p1.enter_context(nc.sbuf_tensor(f"x1r{j}", [128, G, TT1], F32R))
               for j in range(3)]
        h_ext = p1.enter_context(nc.sbuf_tensor("h_ext", [128, G, TT1 + 1], F32))
        xk_bf = p1.enter_context(nc.sbuf_tensor("xk_bf", [128, G, TT1], BF16))
        xv_bf = p1.enter_context(nc.sbuf_tensor("xv_bf", [128, G, TT1], BF16))
        xr_bf = p1.enter_context(nc.sbuf_tensor("xr_bf", [128, G, TT1], BF16))
        ekq = p1.enter_context(nc.sbuf_tensor("ekq", [128, G, 2, TT1 + 1], F32))
        AB = p1.enter_context(nc.sbuf_tensor("AB", [128, G, 2, TT1 + 1], F32))
        er2b = p1.enter_context(nc.sbuf_tensor("er2b", [128, 2, TT1], F32))
        nd = p1.enter_context(nc.sbuf_tensor("nd", [128, 3, 2, TT1], F32))
        den2_b = p1.enter_context(nc.sbuf_tensor("den2_b", [128, 2, TT1], F32))
        rden_b = p1.enter_context(nc.sbuf_tensor("rden_b", [128, 2, TT1], F32))
        dbuf = p1.enter_context(nc.sbuf_tensor("dbuf", [128, TT1], F32))
        y_bf0 = p1.enter_context(nc.sbuf_tensor("y_bf0", [128, G, TT1], BF16))
        x2e = p1.enter_context(nc.sbuf_tensor("x2e", [128, G, TT1], F32))

        with nc.Block() as blk:
            @blk.sync
            def _(sy):
                ii = sy.dma_start(out=pv[:, :, :], in_=pvec[:, :, :])
                cx.mark_dma("ldpv", ii)
                ii = sy.dma_start(out=pvH[:, :], in_=pvecH[:, :])
                cx.mark_dma("ldpv", ii)
                ii = sy.dma_start(out=hm[:, :], in_=hmask[:, :])
                cx.mark_dma("ldpv", ii, "ld_pv")


            @blk.gpsimd
            def _(po):
                ii = po.dma_start(out=wk_s[:, :, :],
                                  in_=wkT.rearrange("(kb p) o -> p kb o", p=128))
                cx.mark_dma("wp1", ii)
                ii = po.dma_start(out=wv_s[:, :, :],
                                  in_=wvT.rearrange("(kb p) o -> p kb o", p=128))
                cx.mark_dma("wp1", ii)
                ii = po.dma_start(out=wr_s[:, :, :],
                                  in_=wrT.rearrange("(kb p) o -> p kb o", p=128))
                cx.mark_dma("wp1", ii)
                ii = po.dma_start(out=wo_s[:, :, :],
                                  in_=woT.rearrange("(kb p) o -> p kb o", p=128))
                cx.mark_dma("wp1", ii, "ld_w1p")
                ii = po.dma_start(out=wfv_b[:, :], in_=wfvT[:, :])
                cx.mark_dma("wcv1", ii, "cv_wfv")
                ii = po.dma_start(out=w2_b[:, :], in_=w2[:, :])
                cx.mark_dma("wcv2", ii, "cv_w2")
                for j in range(2):
                    ii = po.dma_start(out=xin[j][:, :, :],
                                      in_=xT_v[:, :, j * TT1:(j + 1) * TT1])
                    cx.mark_dma(f"xin{j % 2}", ii, f"ldx_{j}")
                ii = po.memset(onesf[:, :], 1.0)
                n0 = cx.mark("pool", ii)
                po.wait_ge(cx.sem["pool"], n0)
                po.tensor_copy(onesr[:, :], onesf[:, :])
                po.memset(onesrowf[0:1, :], 1.0)
                po.tensor_copy(onesrow[0:1, :], onesrowf[0:1, :])
                ii = po.memset(epsb[:, :], EPS)
                cx.mark("pool", ii, "consts")
                cx.ev["rows0"] = ("pool", cx.cnt["pool"])
                po.memset(h_ext[:, :, 0:1], 0.0)
                ii = po.memset(AB[:, :, :, TT1:TT1 + 1], 0.0)
                cx.mark("pool", ii, "zeros")

            @blk.scalar
            def _(s):
                s.add_instruction(mybir.InstLoadActFuncSet(
                    name=nc.get_next_instruction_name(),
                    act_func_set_id=6, ins=[], outs=[]))
                cx.need("s", s, "ld_pv")
                s.activation(eu8[:, :], pv[:, :, IV_TF], AF.Exp)
                ii = s.activation(dec8[:, :], pv[:, :, IV_TD], AF.Exp)
                cx.mark("s", ii, "dec_a")

            @blk.vector
            def _(v):
                cx.need("v", v, "ld_pv")
                cx.need("v", v, "dec_a")
                cx.need("v", v, "consts")
                v.tensor_copy(
                    rows.rearrange("o (s w) -> o s w", w=ROWN)[0:1, :, 257:260],
                    onesrowf[0:1, 0:1].unsqueeze(1).broadcast_to([1, 8, 3]))
                ii = v.tensor_scalar(dec8[:, :], dec8[:, :], -1.0, None, ALU.mult)
                cx.mark("v", ii, "dec_b")

            @blk.scalar
            def _(s):
                cx.need("s", s, "dec_b")
                ii = s.activation(dec8[:, :], dec8[:, :], AF.Exp)
                cx.mark("s", ii, "dec_c")

            @blk.vector
            def _(v):
                cx.need("v", v, "dec_c")
                ii = v.engine_nop()
                cx.mark("v", ii, "dec_ext")

            @blk.scalar
            def _(s):
                cx.need("s", s, "zeros")
                ii = s.dma_start(out=rwkvs_v[:, :, 0:1], in_=h_ext[:, :, 0:1],
                                 allow_slow_non_contiguous=True)
                cx.mark_dma("zc", ii)
                ii = s.dma_start(out=x1s_v[:, :, 0:1], in_=h_ext[:, :, 0:1],
                                 allow_slow_non_contiguous=True)
                cx.mark_dma("zc", ii, "zc_cols")

            for i in range(-3, NT1 + 1):
                if 0 <= i - 1 < NT1:
                    for go in range(G):
                        @blk.tensor
                        def _(t, i=i, go=go):
                            bank = pb[2 + go % 5]
                            if go == 0:
                                cx.need("pe", t, f"y_{i - 1}")
                            if go < 2:
                                cx.need_if("pe", t, f"eka_{i - 1}_{6 + go}")
                            elif go < 4:
                                cx.need_if("pe", t, f"ekv_{i - 1}_{4 + go}")
                            elif go == 4:
                                cx.need_if("pe", t, f"erc_{i - 1}_{G - 1}")
                            else:
                                cx.need("pe", t, f"woev_{i - 1}_{go - 5}")
                            for kb in range(G):
                                mm = t.matmul(bank[:, 0:TT1],
                                              wo_s[:, kb, go * 128:(go + 1) * 128],
                                              y_bf0[:, kb, :],
                                              start=(kb == 0), stop=(kb == G - 1))
                            cx.mark("pe", mm, f"wo_{i - 1}_{go}")

                        @blk.scalar
                        def _(s, i=i, go=go):
                            cx.need("s", s, f"wo_{i - 1}_{go}")
                            if go == 0:
                                cx.dma_war("s", s, "x2st")
                            ii = s.activation(x2e[:, go, :],
                                              pb[2 + go % 5][:, 0:TT1],
                                              AF.Copy)
                            cx.mark("s", ii, f"woev_{i - 1}_{go}")

                    @blk.scalar
                    def _(s, i=i):
                        cx.need("s", s, f"woev_{i - 1}_{G - 1}")
                        ii = s.dma_start(
                            out=rwkvs_v[:, :, 1 + (i - 1) * TT1:1 + i * TT1],
                            in_=x2e[:, :, :])
                        cx.mark_dma("x2st", ii, f"x2st_{i - 1}")

                if 0 <= i + 1 < NT1:
                    j1 = i + 1
                    tag1 = f"l1_{j1}"

                    @blk.tensor
                    def _(t, j1=j1, tag1=tag1):
                        pe_bcast(t, tag1, pb[0], pb[0], TT1, 0,
                                 war_deps=[f"ap_l0_{j1}"], moff=0, ioff=256)

                    @blk.vector
                    def _(v, j1=j1, tag1=tag1):
                        war = [f"hc_{j1}"]
                        if j1 == 0:
                            war.append("zeros")
                        v_apply(v, tag1, pb[0], pb[0], x1r[j1 % 3][:, :, :],
                                h_ext[:, :, 1:TT1 + 1], TT1, war_deps=war,
                                moff=0, ioff=256)

                def emit_aff1(i=i):
                    if 0 <= i + 1 < NT1:
                        tag1 = f"l1_{i + 1}"

                        @blk.scalar
                        def _(s, tag1=tag1):
                            s_affine(s, tag1, h_ext[:, :, 1:TT1 + 1],
                                     h_ext[:, :, 1:TT1 + 1], IV_LN1W, IV_LN1B)

                if 0 <= i < NT1:
                    @blk.vector
                    def _(v, i=i):
                        if i == 0:
                            cx.need("v", v, "zeros")
                            cx.need("v", v, "dec_ext")
                        if DEBUG_TAPS:
                            cx.need_if("v", v, f"ydbg_{i - 1}")
                        ii = v.tensor_copy(ekq[:, :, :, 0:1],
                                           AB[:, :, :, TT1:TT1 + 1])
                        cx.mark("v", ii, f"carry_{i}")

                if 0 <= i + 3 < NT1:
                    j0e = i + 3

                    @blk.gpsimd
                    def _(po, j0e=j0e):
                        cx.need("pool", po, f"ldx_{j0e}")
                        cx.need_if("pool", po, f"st_l0_{j0e - 1}")
                        ii = po.tensor_tensor(sqb[:, :, :], xin[j0e % 2][:, :, :],
                                              xin[j0e % 2][:, :, :], ALU.mult)
                        cx.mark("pool", ii, f"sq0_{j0e}")

                def emit_b0a0(i=i):
                    if 0 <= i + 2 < NT1:
                        tag0b = f"l0_{i + 2}"

                        @blk.tensor
                        def _(t, i=i, tag0b=tag0b):
                            pe_bcast(t, tag0b, pb[0], pb[0], TT1, 4,
                                     war_deps=[f"ap_l1_{i + 1}"], moff=0,
                                     ioff=256)

                        @blk.vector
                        def _(v, i=i, tag0b=tag0b):
                            j0 = i + 2
                            v_apply(v, tag0b, pb[0], pb[0],
                                    xin[j0 % 2][:, :, :],
                                    xin[j0 % 2][:, :, :], TT1,
                                    war_deps=[f"sq0_{j0}", f"st_l0_{j0}"],
                                    moff=0, ioff=256)

                def emit_aff0(i=i):
                    if 0 <= i + 2 < NT1:
                        j0 = i + 2
                        tag0 = f"l0_{j0}"

                        @blk.scalar
                        def _(s, j0=j0, tag0=tag0):
                            war0 = [f"st_l1_{j0 - 3}"]
                            if DEBUG_TAPS:
                                war0.append(f"x1dbg_{j0 - 3}")
                            cx.dma_war("s", s, "x1st")
                            s_affine(s, tag0, xin[j0 % 2][:, :, :],
                                     x1r[j0 % 3][:, :, :], IV_LN0W, IV_LN0B,
                                     war_deps=war0)
                            cx.need_if("s", s, f"st_l1_{j0 - 1}")
                            ii = s.activation(sqb2[:, :, :],
                                              x1r[j0 % 3][:, :, :], AF.Square)
                            cx.mark("s", ii, f"sq1_{j0}")

                        @blk.gpsimd
                        def _(po, j0=j0, tag0=tag0):
                            cx.need("pool", po, f"af_{tag0}")
                            ii = po.dma_start(
                                out=x1s_v[:, :, 1 + j0 * TT1:1 + (j0 + 1) * TT1],
                                in_=x1r[j0 % 3][:, :, :])
                            cx.mark_dma("x1st", ii, f"x1st_{j0}")

                    @blk.gpsimd
                    def _(po, i=i):
                        if 2 <= i + 4 < NT1:
                            j = i + 4
                            cx.need_if("pool", po, f"af_l0_{j - 2}")
                            cx.dma_war("pool", po, f"xin{j % 2}")
                            ii = po.dma_start(
                                out=xin[j % 2][:, :, :],
                                in_=xT_v[:, :, j * TT1:(j + 1) * TT1])
                            cx.mark_dma(f"xin{j % 2}", ii, f"ldx_{j}")

                def emit_st1(i=i):
                    if 0 <= i + 2 < NT1:
                        j0 = i + 2
                        tag0 = f"l0_{j0}"
                        tag1b = f"l1_{j0}"

                        @blk.tensor
                        def _(t, j0=j0, tag0=tag0, tag1b=tag1b):
                            pe_stats(t, tag1b, x1r[j0 % 3][:, :, :],
                                     sqb2[:, :, :], pb[1], pb[1], TT1,
                                     deps=[f"af_{tag0}", f"sq1_{j0}",
                                           f"vr_l1_{j0 - 1}"], xoff=0,
                                     qoff=256)

                def emit_st0(i=i):
                    if 0 <= i + 3 < NT1:
                        j0 = i + 3
                        tag0c = f"l0_{j0}"

                        @blk.tensor
                        def _(t, j0=j0, tag0c=tag0c):
                            pe_stats(t, tag0c, xin[j0 % 2][:, :, :],
                                     sqb[:, :, :], pb[7], pb[7], TT1,
                                     deps=[f"ldx_{j0}", f"sq0_{j0}",
                                           f"vr_l0_{j0 - 1}"], xoff=0,
                                     qoff=256)

                if 0 <= i < NT1:
                    for g in range(G):
                        if g == 2:
                            emit_aff1(i)
                        if g == 4:
                            emit_b0a0(i)
                        if g == 5:
                            emit_aff0(i)

                        @blk.tensor
                        def _(t, i=i, g=g):
                            if i == 0 and g == 0:
                                cx.need("pe", t, "ld_w1p")
                            cx.need("pe", t, f"mx_{i}")
                            bk, bv, br = pb[2 + g % 2], pb[4 + g % 2], pb[6]
                            if g < 2:
                                cx.need_if("pe", t, f"woev_{i - 1}_{5 + g}")
                            cx.need_if("pe", t, f"eka_{i}_{g - 2}")
                            for kb in range(G):
                                mm = t.matmul(bk[:, 0:TT1],
                                              wk_s[:, kb, g * 128:(g + 1) * 128],
                                              xk_bf[:, kb, :], start=(kb == 0),
                                              stop=(kb == G - 1))
                            cx.mark("pe", mm, f"k_{i}_{g}")
                            if g == 0:
                                cx.need_if("pe", t, f"woev_{i - 1}_{7}")
                                cx.need_if("pe", t, f"woev_{i - 1}_{4}")
                            if g == 1:
                                cx.need_if("pe", t, f"woev_{i - 1}_{3}")
                            cx.need_if("pe", t, f"ekv_{i}_{g - 2}")
                            for kb in range(G):
                                mm = t.matmul(bv[:, 0:TT1],
                                              wv_s[:, kb, g * 128:(g + 1) * 128],
                                              xv_bf[:, kb, :], start=(kb == 0),
                                              stop=(kb == G - 1))
                            cx.mark("pe", mm, f"vv_{i}_{g}")
                            cx.need_if("pe", t, f"erc_{i}_{g - 1}")
                            for kb in range(G):
                                mm = t.matmul(br[:, 0:TT1],
                                              wr_s[:, kb, g * 128:(g + 1) * 128],
                                              xr_bf[:, kb, :], start=(kb == 0),
                                              stop=(kb == G - 1))
                            cx.mark("pe", mm, f"r_{i}_{g}")

                        @blk.scalar
                        def _(s, i=i, g=g):
                            cx.need("s", s, f"k_{i}_{g}")
                            if g == 0:
                                cx.need_if("s", s, f"y_{i - 1}")
                                cx.need_if("s", s, f"carry_{i}")
                            ii = s.activation(ekq[:, g, 1, 1:TT1 + 1],
                                              pb[2 + g % 2][:, 0:TT1], AF.Exp)
                            cx.mark("s", ii, f"eka_{i}_{g}")
                            cx.need("s", s, f"r_{i}_{g}")
                            cx.need_if("s", s, f"dn_{i}_{g - 2}")
                            ii = s.activation(er2b[:, g % 2, :], pb[6][:, 0:TT1],
                                              AF.Exp, scale=-1.0)
                            cx.mark("s", ii, f"erc_{i}_{g}")
                            if g >= 1:
                                gq = g - 1
                                cx.need("s", s, f"dn_{i}_{gq}")
                                cx.need_if("s", s, f"yg_{i}_{gq - 1}")
                                s.activation(den2_b[:, gq % 2, :],
                                             den2_b[:, gq % 2, :], AF.Ln)
                                ii = s.activation(rden_b[:, gq % 2, :],
                                                  den2_b[:, gq % 2, :],
                                                  AF.Exp, scale=-1.0)
                                cx.mark("s", ii, f"rdn_{i}_{gq}")

                        @blk.vector
                        def _(v, i=i, g=g):
                            cx.need("v", v, f"eka_{i}_{g}")
                            cx.need("v", v, f"vv_{i}_{g}")
                            ii = v.tensor_tensor(ekq[:, g, 0, 1:TT1 + 1],
                                                 ekq[:, g, 1, 1:TT1 + 1],
                                                 pb[4 + g % 2][:, 0:TT1], ALU.mult)
                            cx.mark("v", ii, f"ekv_{i}_{g}")
                            dec_b = dec8[:, g:g + 1].broadcast_to([128, TT1 + 1])
                            v.tensor_tensor_scan(AB[:, g, 0, :], dec_b,
                                                 ekq[:, g, 0, :], 0.0,
                                                 ALU.mult, ALU.add)
                            ii = v.tensor_tensor_scan(AB[:, g, 1, :], dec_b,
                                                      ekq[:, g, 1, :], 0.0,
                                                      ALU.mult, ALU.add)
                            cx.mark("v", ii, f"scn_{i}_{g}")
                            cx.need_if("v", v, f"yg_{i}_{g - 3}")
                            ii = v.scalar_tensor_tensor(nd[:, g % 3, :, :],
                                                        ekq[:, g, :, 1:TT1 + 1],
                                                        eu8[:, g:g + 1],
                                                        AB[:, g, :, 0:TT1],
                                                        ALU.mult, ALU.add)
                            cx.mark("v", ii, f"nd_{i}_{g}")
                            cx.need("v", v, f"erc_{i}_{g}")
                            cx.need_if("v", v, f"rdn_{i}_{g - 2}")
                            ii = v.scalar_tensor_tensor(den2_b[:, g % 2, :],
                                                        er2b[:, g % 2, :], 1.0,
                                                        nd[:, g % 3, 1, :],
                                                        ALU.add, ALU.mult)
                            cx.mark("v", ii, f"dn_{i}_{g}")

                        @blk.gpsimd
                        def _(po, i=i, g=g):
                            if g >= 1:
                                gq = g - 1
                                cx.need("pool", po, f"rdn_{i}_{gq}")
                                if gq == 0:
                                    cx.need_if("pool", po,
                                               f"wo_{i - 1}_{G - 1}")
                                ii = po.tensor_tensor(y_bf0[:, gq, :],
                                                      nd[:, gq % 3, 0, :],
                                                      rden_b[:, gq % 2, :],
                                                      ALU.mult)
                                cx.mark("pool", ii, f"yg_{i}_{gq}")

                    @blk.scalar
                    def _(s, i=i):
                        gq = G - 1
                        cx.need("s", s, f"dn_{i}_{gq}")
                        cx.need_if("s", s, f"yg_{i}_{gq - 1}")
                        s.activation(den2_b[:, gq % 2, :],
                                     den2_b[:, gq % 2, :], AF.Ln)
                        ii = s.activation(rden_b[:, gq % 2, :],
                                          den2_b[:, gq % 2, :],
                                          AF.Exp, scale=-1.0)
                        cx.mark("s", ii, f"rdn_{i}_{gq}")

                    @blk.gpsimd
                    def _(po, i=i):
                        gq = G - 1
                        cx.need("pool", po, f"rdn_{i}_{gq}")
                        ii = po.tensor_tensor(y_bf0[:, gq, :],
                                              nd[:, gq % 3, 0, :],
                                              rden_b[:, gq % 2, :],
                                              ALU.mult)
                        cx.mark("pool", ii, f"yg_{i}_{gq}")
                        cx.ev[f"y_{i}"] = ("pool", cx.cnt["pool"])

                if DEBUG_TAPS and 0 <= i < NT1:
                    @blk.scalar
                    def _(s, i=i):
                        cx.need("s", s, f"y_{i}")
                        ii = s.dma_start(out=ydbg_v[:, :, i * TT1:(i + 1) * TT1],
                                         in_=y_bf0[:, :, :])
                        cx.mark_dma("ydbg", ii, f"ydbg_{i}")
                        cx.need("s", s, f"mx_{i}")
                        ii = s.dma_start(out=kdbg_v[:, :, i * TT1:(i + 1) * TT1],
                                         in_=xk_bf[:, :, :])
                        cx.mark_dma("ydbg", ii)
                        ii = s.dma_start(out=vdbg_v[:, :, i * TT1:(i + 1) * TT1],
                                         in_=xv_bf[:, :, :])
                        cx.mark_dma("ydbg", ii, f"kvdbg_{i}")
                        cx.need("s", s, f"af_l0_{i}")
                        ii = s.dma_start(out=x1dbg_v[:, :, i * TT1:(i + 1) * TT1],
                                         in_=x1r[i % 3][:, :, :])
                        cx.mark_dma("ydbg", ii, f"x1dbg_{i}")

                if not (0 <= i < NT1):
                    emit_aff1(i)
                    emit_b0a0(i)
                    emit_aff0(i)

                emit_st1(i)
                emit_st0(i)

                if 0 <= i + 2 < NT1:
                    j0 = i + 2
                    tag0 = f"l0_{j0}"
                    tag1b = f"l1_{j0}"

                    @blk.vector
                    def _(v, i=i, tag1b=tag1b):
                        v_rows(v, tag1b, pb[1], pb[1], TT1, 0, xoff=0,
                               qoff=256, war_deps=[f"bc_l1_{i + 1}",
                                                   f"ir_l1_{i + 1}"])

                    @blk.scalar
                    def _(s, i=i, tag1b=tag1b):
                        s_irow(s, tag1b, TT1, 0, war_deps=[f"bc_l1_{i + 1}"])

                if 0 <= i + 3 < NT1:
                    j0 = i + 3
                    tag0c = f"l0_{j0}"

                    @blk.vector
                    def _(v, j0=j0, tag0c=tag0c):
                        v_rows(v, tag0c, pb[7], pb[7], TT1, 4, xoff=0,
                               qoff=256, war_deps=[f"bc_l0_{j0 - 1}",
                                                   f"ir_l0_{j0 - 1}"])

                    @blk.scalar
                    def _(s, j0=j0, tag0c=tag0c):
                        s_irow(s, tag0c, TT1, 4, war_deps=[f"bc_l0_{j0 - 1}"])

                if 0 <= i + 1 < NT1:
                    j1 = i + 1

                    @blk.vector
                    def _(v, i=i, j1=j1):
                        cx.need("v", v, f"af_l1_{j1}")
                        cx.need_if("v", v, f"r_{j1 - 1}_{G - 1}")
                        if DEBUG_TAPS:
                            cx.need_if("v", v, f"kvdbg_{j1 - 1}")
                        ii = None
                        for g in range(G):
                            sh_ap = h_ext[:, g, 0:TT1]
                            v.tensor_tensor(dbuf[:, :], h_ext[:, g, 1:TT1 + 1],
                                            sh_ap, ALU.subtract)
                            v.scalar_tensor_tensor(xk_bf[:, g, :], dbuf[:, :],
                                                   pv[:, g, IV_TMK:IV_TMK + 1],
                                                   sh_ap, ALU.mult, ALU.add)
                            v.scalar_tensor_tensor(xv_bf[:, g, :], dbuf[:, :],
                                                   pv[:, g, IV_TMV:IV_TMV + 1],
                                                   sh_ap, ALU.mult, ALU.add)
                            ii = v.scalar_tensor_tensor(xr_bf[:, g, :], dbuf[:, :],
                                                        pv[:, g, IV_TMR:IV_TMR + 1],
                                                        sh_ap, ALU.mult, ALU.add)
                        cx.mark("v", ii, f"mx_{j1}")
                        ii = v.tensor_copy(h_ext[:, :, 0:1],
                                           h_ext[:, :, TT1:TT1 + 1])
                        cx.mark("v", ii, f"hc_{j1 + 1}")

    with contextlib.ExitStack() as p2:
        wfk_s = p2.enter_context(nc.sbuf_tensor("wfk_s", [128, G, 4 * C], BF16))
        wfr_s = p2.enter_context(nc.sbuf_tensor("wfr_s", [128, G, C], BF16))
        wfvb = [p2.enter_context(nc.sbuf_tensor(f"wfvb{j}", [128, 4, C], BF16))
                for j in range(3)]
        x2in = [p2.enter_context(nc.sbuf_tensor(f"x2in{j}", [128, G, TT2 + 1], F32))
                for j in range(2)]
        rnd2 = p2.enter_context(nc.sbuf_tensor("rnd2", [128, G, TT2 + 2], F32R))
        sq2 = p2.enter_context(nc.sbuf_tensor("sq2", [128, G, TT2 + 2], F32R))
        h2_ext = p2.enter_context(nc.sbuf_tensor("h2_ext", [128, G, TT2 + 1], F32))
        d2 = p2.enter_context(nc.sbuf_tensor("d2", [128, TT2], F32))
        xk2 = p2.enter_context(nc.sbuf_tensor("xk2", [128, G, TT2], BF16))
        xr2 = p2.enter_context(nc.sbuf_tensor("xr2", [128, G, TT2], BF16))
        rwin = p2.enter_context(
            nc.sbuf_tensor("rwin", [128, G, TT2 + 1], F32))
        r3 = p2.enter_context(nc.sbuf_tensor("r3", [128, G, TT2], F32))
        kf = p2.enter_context(nc.sbuf_tensor("kf", [128, GH, TT2], BF16))
        relu_s = [p2.enter_context(nc.sbuf_tensor(f"relu{j}", [128, TT2], F32))
                  for j in range(2)]
        zpad = p2.enter_context(nc.sbuf_tensor("zpad", [128, G], F32))
        rpr = p2.enter_context(nc.sbuf_tensor("rpr", [128, G, TT2], F32))

        wfv_bv = wfv_b.rearrange("(cc f p) o -> p (cc f) o", p=128, f=4)

        with nc.Block() as blk:
            @blk.gpsimd
            def _(po):
                cx.dma_war("pool", po, "x2st")
                cx.dma_war("pool", po, "x1st")
                ii = po.memset(zpad[:, :], 0.0)
                cx.mark("pool", ii, "zp_a")
                ii = po.dma_start(out=wfr_s[:, :, :],
                                  in_=wfrT.rearrange("(kb p) o -> p kb o", p=128))
                cx.mark_dma("wfrld", ii, "ld_wfr")
                wfk_v = wfkT.rearrange("(kb p) o -> p kb o", p=128)
                ii = po.dma_start(out=wfk_s[:, :, 0:2 * C],
                                  in_=wfk_v[:, :, 0:2 * C])
                cx.mark_dma("wfkld", ii, "ld_wfk")
                ii = po.dma_start(out=wfk_s[:, :, 2 * C:4 * C],
                                  in_=wfk_v[:, :, 2 * C:4 * C])
                cx.mark_dma("wfkld", ii, "ld_wfk2")

            @blk.vector
            def _(v):
                cx.need("v", v, "zp_a")
                v.tensor_copy(rnd2[:, :, TT2 + 1:TT2 + 2],
                              zpad[:, :].unsqueeze(-1))
                ii = v.tensor_copy(sq2[:, :, TT2 + 1:TT2 + 2],
                                   zpad[:, :].unsqueeze(-1))
                cx.mark("v", ii, "p2pad")

            for jj in range(-1, NT2):
                @blk.sync
                def _(sy, jj=jj):
                    if 0 <= jj + 1 < NT2:
                        j = jj + 1
                        if j <= 1:
                            cx.dma_war("sync", sy, "x2st")
                            cx.dma_war("sync", sy, "x1st")
                            cx.dma_war("sync", sy, "zc")
                        cx.need_if("sync", sy, f"rpr_{j - 2}")
                        cx.need_if("sync", sy, f"x2add_{j - 1}")
                        cx.dma_war("sync", sy, f"x2in{j % 2}")
                        cx.dma_war("sync", sy, "rwin")
                        if "hreg" not in hoff_reg:
                            hoff_reg["hreg"] = sy.alloc_register("hreg")
                            hoff_reg["oreg"] = sy.alloc_register("oreg")
                            pid = sy.partition_id()
                            hsv = sy.snap((pid % 2) * TO, min_val=0, max_val=TO)
                            sy.reg_mov(hoff_reg["hreg"], hsv)
                        sy.reg_add(hoff_reg["oreg"], hoff_reg["hreg"], j * TT2)
                        off = sy.snap(hoff_reg["oreg"], min_val=j * TT2,
                                      max_val=TO + j * TT2)
                        ii = sy.dma_start(out=x2in[j % 2][:, :, :],
                                          in_=x1s_v[:, :, bass.ds(off, TT2 + 1)])
                        cx.mark_dma(f"x2in{j % 2}", ii, f"ldx2_{j}")
                        ii = sy.dma_start(out=rwin[:, :, :],
                                          in_=rwkvs_v[:, :,
                                                      bass.ds(off, TT2 + 1)])
                        cx.mark_dma("rwin", ii, f"ldrw_{j}")
                    if 0 <= jj < NT2:
                        for ch in range(3):
                            cx.need_if("sync", sy,
                                       f"kvch_{jj - 1}_{[6, 7, 5][ch]}")
                            if ch == 0 and jj == 0:
                                cx.need("sync", sy, "cv_wfv")
                            cx.dma_war("sync", sy, f"wfv{ch % 3}")
                            ii = sy.dma_start(out=wfvb[ch % 3][:, :, :],
                                              in_=wfv_bv[:, 4 * ch:4 * ch + 4, :])
                            cx.mark_dma(f"wfv{ch % 3}", ii, f"wfv_{jj}_{ch}")

                if 0 <= jj + 1 < NT2:
                    j = jj + 1
                    tag2 = f"l2_{j}"

                    @blk.vector
                    def _(v, j=j):
                        cx.need("v", v, f"ldx2_{j}")
                        cx.need("v", v, f"ldrw_{j}")
                        cx.need_if("v", v, f"rpr_{j - 2}")
                        ii = v.tensor_tensor(x2in[j % 2][:, :, :],
                                             x2in[j % 2][:, :, :],
                                             rwin[:, :, :], ALU.add)
                        cx.mark("v", ii, f"x2add_{j}")

                    @blk.gpsimd
                    def _(po, j=j):
                        cx.need("pool", po, f"x2add_{j}")
                        cx.need_if("pool", po, f"st_l2_{j - 1}")
                        ii = po.tensor_copy(rnd2[:, :, 0:TT2 + 1],
                                            x2in[j % 2][:, :, :])
                        cx.mark("pool", ii, f"rd2_{j}")
                        ii = po.tensor_tensor(sq2[:, :, 0:TT2 + 1],
                                              x2in[j % 2][:, :, :],
                                              x2in[j % 2][:, :, :], ALU.mult)
                        cx.mark("pool", ii, f"sq2_{j}")

                def emit_ln2_tail(jj=jj):
                    if not (0 <= jj + 1 < NT2):
                        return
                    j = jj + 1
                    tag2 = f"l2_{j}"

                    @blk.tensor
                    def _(t, j=j, tag2=tag2, jj=jj):
                        pe_stats(t, tag2, rnd2[:, :, :], sq2[:, :, :], pb[0],
                                 pb[1], TT2 + 2,
                                 deps=[f"rd2_{j}", f"sq2_{j}", f"vr_l2_{j - 1}",
                                       f"kv_ev_{jj - 1}_1", "p2pad"])

                    @blk.vector
                    def _(v, j=j, tag2=tag2):
                        v_rows(v, tag2, pb[0], pb[1], TT2 + 1, 0,
                               war_deps=[f"bc_l2_{j - 1}", f"ir_l2_{j - 1}"])

                    @blk.scalar
                    def _(s, j=j, tag2=tag2):
                        s_irow(s, tag2, TT2 + 1, 0,
                               war_deps=[f"bc_l2_{j - 1}"])

                    @blk.tensor
                    def _(t, tag2=tag2, jj=jj):
                        pe_bcast(t, tag2, pb[2], pb[3], TT2 + 2, 0,
                                 war_deps=[f"ap_l2_{jj}", f"kv_ev_{jj - 1}_3"])

                    @blk.vector
                    def _(v, j=j, tag2=tag2):
                        v_apply_affine(v, tag2, pb[2], pb[3],
                                       x2in[j % 2][:, :, :], h2_ext[:, :, :],
                                       TT2 + 1, IV_LN2W, IV_LN2B,
                                       war_deps=[f"mx2_{j - 1}"])

                def emit_mx2(jj=jj):
                    if not (0 <= jj + 1 < NT2):
                        return

                    @blk.vector
                    def _(v, jj=jj):
                        j = jj + 1
                        cx.need("v", v, f"af_l2_{j}")
                        if j == 0:
                            v.tensor_scalar(h2_ext[:, :, 0:1],
                                            h2_ext[:, :, 0:1],
                                            hm[:, 0:1], None, ALU.mult)
                        cx.need_if("v", v, f"fk_{jj}_{GH - 1}")
                        cx.need_if("v", v, f"fr_{jj}_{G - 1}")
                        if DEBUG_TAPS:
                            cx.need_if("v", v, f"p2dbg_{jj}")
                        ii = None
                        for g in range(G):
                            sh_ap = h2_ext[:, g, 0:TT2]
                            v.tensor_tensor(d2[:, :], h2_ext[:, g, 1:TT2 + 1],
                                            sh_ap, ALU.subtract)
                            v.scalar_tensor_tensor(xk2[:, g, :], d2[:, :],
                                                   pv[:, g, IV_CMK:IV_CMK + 1],
                                                   sh_ap, ALU.mult, ALU.add)
                            ii = v.scalar_tensor_tensor(xr2[:, g, :], d2[:, :],
                                                        pv[:, g, IV_CMR:IV_CMR + 1],
                                                        sh_ap, ALU.mult, ALU.add)
                        cx.mark("v", ii, f"mx2_{j}")

                if not (0 <= jj < NT2):
                    emit_ln2_tail(jj)
                    emit_mx2(jj)

                if 0 <= jj < NT2:
                    for go in range(G):
                        @blk.tensor
                        def _(t, jj=jj, go=go):
                            if jj == 0 and go == 0:
                                cx.need("pe", t, "ld_wfr")
                            if go == 0:
                                cx.need("pe", t, f"mx2_{jj}")
                            bank = pb[4 + go % 2]
                            if go < 2:
                                cx.need_if("pe", t, f"kv_ev_{jj - 1}_{4 + go}")
                            else:
                                cx.need("pe", t, f"er2_{jj}_{go - 2}")
                            for kb in range(G):
                                mm = t.matmul(bank[:, 0:TT2],
                                              wfr_s[:, kb, go * 128:(go + 1) * 128],
                                              xr2[:, kb, :], start=(kb == 0),
                                              stop=(kb == G - 1))
                            cx.mark("pe", mm, f"fr_{jj}_{go}")

                        @blk.scalar
                        def _(s, jj=jj, go=go):
                            cx.need("s", s, f"fr_{jj}_{go}")
                            cx.need_if("s", s, f"kv_ev_{jj - 1}_{go}")
                            ii = s.activation(r3[:, go, :],
                                              pb[4 + go % 2][:, 0:TT2],
                                              AF.Exp, scale=-1.0)
                            cx.mark("s", ii, f"er2_{jj}_{go}")

                    @blk.vector
                    def _(v, jj=jj):
                        cx.need("v", v, f"er2_{jj}_{G - 1}")
                        ii = v.tensor_scalar(r3[:, :, :], r3[:, :, :], 1.0,
                                             None, ALU.add)
                        cx.mark("v", ii, f"r3a_{jj}")

                    @blk.scalar
                    def _(s, jj=jj):
                        cx.need("s", s, f"r3a_{jj}")
                        s.activation(r3[:, :, :], r3[:, :, :], AF.Ln)
                        ii = s.activation(r3[:, :, :], r3[:, :, :], AF.Exp,
                                          scale=-1.0)
                        cx.mark("s", ii, f"r3_{jj}")

                    for gh in range(GH):
                        if gh == 16:
                            emit_ln2_tail(jj)
                        @blk.tensor
                        def _(t, jj=jj, gh=gh):
                            if jj == 0 and gh == 0:
                                cx.need("pe", t, "ld_wfk")
                            if jj == 0 and gh == 16:
                                cx.need("pe", t, "ld_wfk2")
                            bank = pb[6 + gh % 2]
                            if gh < 2:
                                cx.need_if("pe", t, f"kv_ev_{jj - 1}_{6 + gh}")
                            else:
                                cx.need("pe", t, f"fkm_{jj}_{gh - 2}")
                            for kb in range(G):
                                mm = t.matmul(bank[:, 0:TT2],
                                              wfk_s[:, kb, gh * 128:(gh + 1) * 128],
                                              xk2[:, kb, :], start=(kb == 0),
                                              stop=(kb == G - 1))
                            cx.mark("pe", mm, f"fk_{jj}_{gh}")

                        @blk.scalar
                        def _(s, jj=jj, gh=gh):
                            cx.need("s", s, f"fk_{jj}_{gh}")
                            cx.need_if("s", s, f"fkm_{jj}_{gh - 2}")
                            ii = s.activation(relu_s[gh % 2][:, :],
                                              pb[6 + gh % 2][:, 0:TT2], AF.Relu)
                            cx.mark("s", ii, f"rl_{jj}_{gh}")

                        @blk.vector
                        def _(v, jj=jj, gh=gh):
                            cx.need("v", v, f"rl_{jj}_{gh}")
                            cx.need_if("v", v, f"kvl_{jj - 1}")
                            if DEBUG_TAPS:
                                cx.need_if("v", v, f"p2dbg_{jj - 1}")
                            ii = v.tensor_tensor(kf[:, gh, :], relu_s[gh % 2][:, :],
                                                 pb[6 + gh % 2][:, 0:TT2], ALU.mult)
                            cx.mark("v", ii, f"fkm_{jj}_{gh}")

                    for ch in range(8):
                        @blk.tensor
                        def _(t, jj=jj, ch=ch):
                            if ch == 0:
                                cx.need_if("pe", t, f"vr_l2_{jj + 1}")
                                cx.need_if("pe", t, f"ap_l2_{jj + 1}")
                                cx.need("pe", t, f"er2_{jj}_{G - 1}")
                                cx.need("pe", t, f"fkm_{jj}_{GH - 1}")
                            cx.need("pe", t, f"wfv_{jj}_{ch}")
                            for sl in range(4):
                                hb = ch * 4 + sl
                                for go in range(G):
                                    mm = t.matmul(pb[go][:, 0:TT2],
                                                  wfvb[ch % 3][:, sl,
                                                               go * 128:(go + 1) * 128],
                                                  kf[:, hb, :], start=(hb == 0),
                                                  stop=(hb == GH - 1))
                                    if hb == GH - 1:
                                        cx.mark("pe", mm, f"kvf_{jj}_{go}")
                            if ch != 7:
                                cx.mark("pe", mm, f"kvch_{jj}_{ch}")
                            else:
                                cx.ev[f"kvch_{jj}_7"] = ("pe", cx.cnt["pe"])
                                cx.ev[f"kvl_{jj}"] = ("pe", cx.cnt["pe"])

                        if ch < 5:
                            @blk.sync
                            def _(sy, jj=jj, ch=ch):
                                cx.need("sync", sy, f"kvch_{jj}_{ch}")
                                cx.dma_war("sync", sy, f"wfv{ch % 3}")
                                ii = sy.dma_start(
                                    out=wfvb[ch % 3][:, :, :],
                                    in_=wfv_bv[:, 4 * (ch + 3):4 * (ch + 3) + 4, :])
                                cx.mark_dma(f"wfv{ch % 3}", ii,
                                            f"wfv_{jj}_{ch + 3}")

                    emit_mx2(jj)

                    @blk.vector
                    def _(v, jj=jj):
                        cx.need("v", v, f"r3_{jj}")
                        cx.need_if("v", v, f"rst_{jj - 1}")
                        ii = None
                        for go in (4, 5, 6, 7, 0, 1, 2, 3):
                            cx.need("v", v, f"kvf_{jj}_{go}")
                            v.tensor_tensor(rpr[:, go, :], r3[:, go, :],
                                            pb[go][:, 0:TT2], ALU.mult)
                            ii = v.tensor_tensor(rpr[:, go, :], rpr[:, go, :],
                                                 x2in[jj % 2][:, go, 1:TT2 + 1],
                                                 ALU.add)
                            cx.mark("v", ii, f"kv_ev_{jj}_{go}")
                        cx.ev[f"rpr_{jj}"] = ("v", cx.cnt["v"])

                    @blk.scalar
                    def _(s, jj=jj):
                        cx.need("s", s, f"rpr_{jj}")
                        ii = s.dma_start(out=rs_v[:, :, jj * TT2:(jj + 1) * TT2],
                                         in_=rpr[:, :, :])
                        cx.mark_dma("rst", ii, f"rst_{jj}")
                        if DEBUG_TAPS:
                            ii = s.dma_start(
                                out=kfdbg_v[:, :, jj * TT2:(jj + 1) * TT2],
                                in_=kf[:, :, :])
                            cx.mark_dma("ydbg", ii)

                if DEBUG_TAPS and 0 <= jj < NT2:
                    @blk.scalar
                    def _(s, jj=jj):
                        cx.need("s", s, f"fk_{jj}_{GH - 1}")
                        ii = s.dma_start(
                            out=k2dbg_v[:, :, jj * TT2:(jj + 1) * TT2],
                            in_=xk2[:, :, :])
                        cx.mark_dma("ydbg", ii, f"p2dbg_{jj}")


    with contextlib.ExitStack() as p3:
        w1_s = p3.enter_context(nc.sbuf_tensor("w1_s", [128, G, 4 * C], BF16))
        w2b_s = [p3.enter_context(nc.sbuf_tensor(f"w2b{j}", [128, 4, C], BF16))
                 for j in range(3)]
        rin = [p3.enter_context(nc.sbuf_tensor(f"rin{j}", [128, G, TT3], F32))
               for j in range(2)]
        x0in = p3.enter_context(nc.sbuf_tensor("x0in", [128, G, TT3], F32))
        rq3 = p3.enter_context(nc.sbuf_tensor("rq3", [128, G, 2 * TT3], F32R))
        x3 = [p3.enter_context(nc.sbuf_tensor(f"x3_{j}", [128, G, TT3], F32))
              for j in range(2)]
        x3b = [p3.enter_context(nc.sbuf_tensor(f"x3b{j}", [128, G, TT3], BF16))
               for j in range(2)]
        hf = p3.enter_context(nc.sbuf_tensor("hf", [128, GH, TT3], BF16))
        x4p = [p3.enter_context(nc.sbuf_tensor(f"x4p{j}", [128, G, TT3], F32))
               for j in range(2)]
        outb = p3.enter_context(nc.sbuf_tensor("outb", [128, G, TT3], F32))

        w2_bv = w2_b.rearrange("(cc f p) o -> p (cc f) o", p=128, f=4)

        with nc.Block() as blk:
            @blk.gpsimd
            def _(po):
                cx.need("pool", po, f"rst_{NT2 - 1}")
                ii = po.dma_start(out=w1_s[:, :, :],
                                  in_=w1.rearrange("(kb p) o -> p kb o", p=128))
                cx.mark_dma("w1ld", ii, "ld_w1")

            for jj in range(-1, NT3 + 2):
                @blk.sync
                def _(sy, jj=jj):
                    if 0 <= jj + 1 < NT3:
                        j = jj + 1
                        if j <= 1:
                            cx.need("sync", sy, f"rst_{NT2 - 1}")
                        cx.need_if("sync", sy, f"x4ev_{j - 2}_{G - 1}")
                        cx.need_if("sync", sy, f"x3pre_{j - 1}")
                        cx.dma_war("sync", sy, f"rin{j % 2}")
                        cx.dma_war("sync", sy, "xo")
                        ii = sy.dma_start(out=rin[j % 2][:, :, :],
                                          in_=rs_v[:, :, j * TT3:(j + 1) * TT3])
                        cx.mark_dma(f"rin{j % 2}", ii, f"ldr_{j}")
                        ii = sy.dma_start(out=x0in[:, :, :],
                                          in_=xTo_v[:, :, j * TT3:(j + 1) * TT3])
                        cx.mark_dma("xo", ii, f"ldo_{j}")
                    if 0 <= jj < NT3:
                        for ch in range(3):
                            cx.need_if("sync", sy,
                                       f"w2ch_{jj - 1}_{[6, 7, 5][ch]}")
                            if ch == 0 and jj == 0:
                                cx.need("sync", sy, "cv_w2")
                            cx.dma_war("sync", sy, f"w2s{ch % 3}")
                            ii = sy.dma_start(out=w2b_s[ch % 3][:, :, :],
                                              in_=w2_bv[:, 4 * ch:4 * ch + 4, :])
                            cx.mark_dma(f"w2s{ch % 3}", ii, f"w2s_{jj}_{ch}")

                if 0 <= jj - 2 < NT3:
                    j2 = jj - 2
                    tagB = f"lB_{j2}"

                    @blk.tensor
                    def _(t, j2=j2, tagB=tagB, jj=jj):
                        for lbl in (f"rd4_{j2}", f"sq4_{j2}",
                                    f"x4ev_{jj - 1}_3",
                                    f"vr_lB_{j2 - 1}", f"ap_lA_{jj}"):
                            cx.need_if("pe", t, lbl)
                        for kb in range(G):
                            mm = t.matmul(pb[2][0:1, 0:2 * TT3], onesr[:, 0:1],
                                          rq3[:, kb, :], start=(kb == 0),
                                          stop=(kb == G - 1))
                        cx.mark("pe", mm, f"st_{tagB}")

                    @blk.vector
                    def _(v, tagB=tagB):
                        v_rows(v, tagB, pb[2], pb[2], TT3, 4, xoff=0, qoff=TT3)

                    @blk.scalar
                    def _(s, tagB=tagB):
                        s_irow(s, tagB, TT3, 4)

                    @blk.tensor
                    def _(t, tagB=tagB):
                        pe_bcast(t, tagB, pb[2], pb[3], TT3, 4,
                                 war_deps=[f"vr_{tagB}"])

                    @blk.vector
                    def _(v, j2=j2, tagB=tagB):
                        v_apply(v, tagB, pb[2], pb[3], x4p[j2 % 2][:, :, :],
                                outb[:, :, :], TT3, war_deps=[f"ost_{j2 - 1}"])

                    @blk.scalar
                    def _(s, j2=j2, tagB=tagB):
                        cx.need("s", s, f"ap_{tagB}")
                        ii = s.dma_start(
                            out=outT_v[:, :, j2 * TT3:(j2 + 1) * TT3],
                            in_=outb[:, :, :])
                        cx.mark_dma("ost", ii, f"ost_{j2}")

                if 0 <= jj + 1 < NT3:
                    j = jj + 1

                    @blk.gpsimd
                    def _(po, j=j):
                        cx.need("pool", po, f"ldr_{j}")
                        cx.need("pool", po, f"ldo_{j}")
                        ii = po.tensor_tensor(rin[j % 2][:, :, :],
                                              rin[j % 2][:, :, :],
                                              x0in[:, :, :], ALU.add)
                        cx.mark("pool", ii, f"x3pre_{j}")
                        cx.need_if("pool", po, f"st_lB_{j - 3}")
                        ii = po.tensor_copy(rq3[:, :, 0:TT3],
                                            rin[j % 2][:, :, :])
                        cx.mark("pool", ii, f"rd3_{j}")
                        ii = po.tensor_tensor(rq3[:, :, TT3:2 * TT3],
                                              rin[j % 2][:, :, :],
                                              rin[j % 2][:, :, :], ALU.mult)
                        cx.mark("pool", ii, f"sq3_{j}")

                def emit_lnA_tail(jj=jj):
                    if not (0 <= jj + 1 < NT3):
                        return
                    j = jj + 1
                    tagA = f"lA_{j}"

                    @blk.tensor
                    def _(t, j=j, tagA=tagA, jj=jj):
                        for lbl in (f"rd3_{j}", f"sq3_{j}", f"vr_lA_{j - 1}",
                                    f"x4ev_{jj - 1}_1", "consts"):
                            cx.need_if("pe", t, lbl)
                        for kb in range(G):
                            mm = t.matmul(pb[0][0:1, 0:2 * TT3], onesr[:, 0:1],
                                          rq3[:, kb, :], start=(kb == 0),
                                          stop=(kb == G - 1))
                        cx.mark("pe", mm, f"st_{tagA}")

                    @blk.vector
                    def _(v, j=j, tagA=tagA):
                        v_rows(v, tagA, pb[0], pb[0], TT3, 0, xoff=0, qoff=TT3,
                               war_deps=[f"bc_lA_{j - 1}", f"ir_lA_{j - 1}"])

                    @blk.scalar
                    def _(s, j=j, tagA=tagA):
                        s_irow(s, tagA, TT3, 0, war_deps=[f"bc_lA_{j - 1}"])

                    @blk.tensor
                    def _(t, tagA=tagA, jj=jj):
                        pe_bcast(t, tagA, pb[2], pb[3], TT3, 0,
                                 war_deps=[f"ap_lB_{jj - 2}", f"x4ev_{jj - 1}_3"])

                    @blk.vector
                    def _(v, j=j, tagA=tagA):
                        v_apply(v, tagA, pb[2], pb[3], rin[j % 2][:, :, :],
                                x3[j % 2][:, :, :], TT3,
                                war_deps=[f"x4ev_{j - 2}_{G - 1}"])

                    @blk.vector
                    def _(v, j=j, tagA=tagA):
                        cx.need_if("v", v, f"w1l_{j - 2}")
                        ii = v.tensor_copy(x3b[j % 2][:, :, :],
                                           x3[j % 2][:, :, :])
                        cx.mark("v", ii, f"x3b_{j}")

                if not (0 <= jj < NT3):
                    emit_lnA_tail(jj)

                if 0 <= jj < NT3:
                    for gh in range(GH):
                        if gh == 8:
                            emit_lnA_tail(jj)
                        @blk.tensor
                        def _(t, jj=jj, gh=gh):
                            if jj == 0 and gh == 0:
                                cx.need("pe", t, "ld_w1")
                            if gh == 0:
                                cx.need("pe", t, f"x3b_{jj}")
                            bank = pb[4 + gh % 2]
                            if gh < 2:
                                cx.need_if("pe", t, f"x4ev_{jj - 1}_{4 + gh}")
                            else:
                                cx.need("pe", t, f"hf_{jj}_{gh - 2}")
                            for kb in range(G):
                                mm = t.matmul(bank[:, 0:TT3],
                                              w1_s[:, kb, gh * 128:(gh + 1) * 128],
                                              x3b[jj % 2][:, kb, :],
                                              start=(kb == 0), stop=(kb == G - 1))
                            mk = f"w1l_{jj}" if gh == GH - 1 else None
                            cx.mark("pe", mm, f"w1_{jj}_{gh}")
                            if mk:
                                cx.ev[mk] = ("pe", cx.cnt["pe"])

                        @blk.scalar
                        def _(s, jj=jj, gh=gh):
                            cx.need("s", s, f"w1_{jj}_{gh}")
                            cx.need_if("s", s, f"w2hb_{jj - 1}_{gh}")
                            ii = s.activation(hf[:, gh, :],
                                              pb[4 + gh % 2][:, 0:TT3],
                                              AF.Relu, bias=pvH[:, gh:gh + 1])
                            cx.mark("s", ii, f"hf_{jj}_{gh}")

                    for ch in range(8):
                        @blk.tensor
                        def _(t, jj=jj, ch=ch):
                            if ch == 0:
                                cx.need_if("pe", t, f"vr_lA_{jj + 1}")
                                cx.need_if("pe", t, f"ap_lA_{jj + 1}")
                                cx.need("pe", t, f"hf_{jj}_{GH - 1}")
                                cx.need_if("pe", t, f"x4ev_{jj - 1}_{G - 1}")
                            cx.need("pe", t, f"w2s_{jj}_{ch}")
                            for sl in range(4):
                                hb = ch * 4 + sl
                                for go in range(G):
                                    mm = t.matmul(pb[go][:, 0:TT3],
                                                  w2b_s[ch % 3][:, sl,
                                                                go * 128:(go + 1) * 128],
                                                  hf[:, hb, :], start=(hb == 0),
                                                  stop=(hb == GH - 1))
                                    if hb == GH - 1:
                                        cx.mark("pe", mm, f"w2o_{jj}_{go}")
                            if ch != 7:
                                cx.mark("pe", mm, f"w2ch_{jj}_{ch}")
                            else:
                                cx.ev[f"w2ch_{jj}_7"] = ("pe", cx.cnt["pe"])
                        if ch < 5:
                            @blk.sync
                            def _(sy, jj=jj, ch=ch):
                                cx.need("sync", sy, f"w2ch_{jj}_{ch}")
                                cx.dma_war("sync", sy, f"w2s{ch % 3}")
                                ii = sy.dma_start(
                                    out=w2b_s[ch % 3][:, :, :],
                                    in_=w2_bv[:, 4 * (ch + 3):4 * (ch + 3) + 4, :])
                                cx.mark_dma(f"w2s{ch % 3}", ii,
                                            f"w2s_{jj}_{ch + 3}")

                    @blk.tensor
                    def _(t, jj=jj):
                        cx.ev[f"w2hb_{jj}_0"] = ("pe", cx.cnt["pe"])
                        for gh in range(GH):
                            cx.ev[f"w2hb_{jj}_{gh}"] = ("pe", cx.cnt["pe"])

                    @blk.vector
                    def _(v, jj=jj):
                        cx.need_if("v", v, f"ost_{jj - 2}")
                        ii = None
                        for go in range(G):
                            cx.need("v", v, f"w2o_{jj}_{go}")
                            ii = v.scalar_tensor_tensor(
                                x4p[jj % 2][:, go, :], pb[go][:, 0:TT3],
                                pv[:, go, IV_B2:IV_B2 + 1],
                                x3[jj % 2][:, go, :], ALU.add, ALU.add)
                            cx.mark("v", ii, f"x4ev_{jj}_{go}")
                        cx.ev[f"x4_{jj}"] = ("v", cx.cnt["v"])

                if 0 <= jj - 1 < NT3:
                    j = jj - 1
                    tagB = f"lB_{j}"

                    @blk.vector
                    def _(v, j=j):
                        cx.need("v", v, f"x4_{j}")
                        cx.need_if("v", v, f"st_lA_{j + 2}")
                        ii = v.tensor_copy(rq3[:, :, 0:TT3],
                                           x4p[j % 2][:, :, :])
                        cx.mark("v", ii, f"rd4_{j}")

                    @blk.scalar
                    def _(s, j=j):
                        cx.need("s", s, f"x4_{j}")
                        cx.need_if("s", s, f"st_lA_{j + 2}")
                        ii = s.activation(rq3[:, :, TT3:2 * TT3],
                                          x4p[j % 2][:, :, :], AF.Square)
                        cx.mark("s", ii, f"sq4_{j}")


        with nc.Block() as blkf:
            @blkf.gpsimd
            def _(po):
                prod, val = cx.ev[f"ost_{NT3 - 1}"]
                po.wait_ge(cx.sem[prod], val)

    cx.es.close()
    return nc


def kernel(**inputs):
    x = np.ascontiguousarray(np.asarray(inputs["L0_attn_x"], np.float32))

    def t_(a):
        return np.ascontiguousarray(np.asarray(a, np.float32).T)

    wk_t = t_(inputs["L0_attn_Wk"])
    wv_t = t_(inputs["L0_attn_Wv"])
    wr_t = t_(inputs["L0_attn_Wr"])
    wo_t = t_(inputs["L0_attn_Wo"])
    wfk_t = t_(inputs["L0_attn_Wfk"])
    wfv_t = t_(inputs["L0_attn_Wfv"])
    wfr_t = t_(inputs["L0_attn_Wfr"])
    w1_a = np.ascontiguousarray(np.asarray(inputs["L0_ffn_W1"], np.float32))
    w2_a = np.ascontiguousarray(np.asarray(inputs["L0_ffn_W2"], np.float32))

    def pv_pack(names):
        arr = np.zeros((128, G, len(names)), np.float32)
        for i, nm in enumerate(names):
            v = np.asarray(inputs[nm], np.float32).reshape(G, 128)
            arr[:, :, i] = v.T
        return np.ascontiguousarray(arr)

    pvec = pv_pack(["L0_attn_ln0_w", "L0_attn_ln0_b", "L0_attn_ln1_w",
                    "L0_attn_ln1_b", "L0_attn_ln2_w", "L0_attn_ln2_b",
                    "L0_attn_time_decay", "L0_attn_time_first", "L0_attn_tmk",
                    "L0_attn_tmv", "L0_attn_tmr", "L0_attn_cmk",
                    "L0_attn_cmr", "L0_ffn_b2"])
    b1 = np.asarray(inputs["L0_ffn_b1"], np.float32).reshape(GH, 128)
    pvecH = np.ascontiguousarray(b1.T)

    xT_all = [np.ascontiguousarray(x[b].T) for b in range(B)]

    in_maps = []
    for core in range(8):
        b, h = core // 2, core % 2
        in_maps.append({
            "xT": xT_all[b],
            "xTo": np.ascontiguousarray(xT_all[b][:, h * TO:(h + 1) * TO]),
            "wkT": wk_t, "wvT": wv_t, "wrT": wr_t, "woT": wo_t,
            "wfkT": wfk_t, "wfvT": wfv_t, "wfrT": wfr_t,
            "w1": w1_a, "w2": w2_a,
            "pvec": pvec, "pvecH": pvecH,
            "hmask": np.full((128, 1), float(h), np.float32),
        })

    nc = build_program()
    res = run_bass_kernel_spmd(nc, in_maps, core_ids=list(range(8)))

    out = np.empty((B, T, C), np.float32)
    for core in range(8):
        b, h = core // 2, core % 2
        out[b, h * TO:(h + 1) * TO, :] = np.asarray(
            res.results[core]["outT"], np.float32).T
    return out


if __name__ == "__main__":
    nc = build_program()
    print("build ok; instructions:",
          sum(1 for _ in nc.all_instructions()) if hasattr(nc, "all_instructions") else "?")
